# version 05
# baseline (speedup 1.0000x reference)
import numpy as np
import ml_dtypes
from contextlib import ExitStack

import concourse.bass as bass
import concourse.mybir as mybir
from concourse.bass_utils import run_bass_kernel_spmd

F32 = mybir.dt.float32
BF16 = mybir.dt.bfloat16
AF = mybir.ActivationFunctionType
ALU = mybir.AluOpType
AX = mybir.AxisListType

S128 = 0.08838834764831845
NHC = 86
TA, TB = 16, 32
EPS = 1e-5

GP = "gpsimd"
SY = "sync"

DMA_SEMS = {
    "WL": GP, "GS": GP, "AL": GP,
    "WA0": GP, "WA1": GP, "GA0": GP, "GA1": GP,
    "WB0": GP, "WB1": GP, "WB2": GP, "WB3": GP, "BB0": GP, "BB1": GP,
    "WS0": GP, "WS1": GP,
    "WQ0": GP, "WQ1": GP, "WQ2": GP, "WQ3": GP, "KS0": GP, "KS1": GP,
    "VS0": GP, "VS1": GP,
    "XR0": SY, "XR1": SY, "XR2": SY, "XR3": SY, "GL0": SY, "GL1": SY, "GL2": SY, "GL3": SY, "KV0": SY, "KV1": SY,
    "XO0": SY, "XO1": SY, "XL": SY, "XS": SY,
    "XP0": "scalar", "XP1": "scalar", "XQ0": "scalar", "XQ1": "scalar",
}
COMP_SEMS = ["MM", "AC", "DV", "MF", "PH"]


class Walk:
    def __init__(self, nc, cur, eng, H):
        self.nc = nc
        self.cur = cur
        self.e = eng
        self.H = H
        self.n = {s: 0 for s in list(DMA_SEMS) + COMP_SEMS}

    def _sem(self, name):
        return self.H["sems"][name]

    def wait(self, eng, sem, val):
        if val is None or val <= 0:
            return
        if self.cur == eng:
            self.e.wait_ge(self._sem(sem), val)

    def dma(self, eng, out, in_, sem):
        assert DMA_SEMS[sem] == eng, (sem, eng)
        if self.cur == eng:
            self.e.dma_start(out=out, in_=in_).then_inc(self._sem(sem), 16)
        self.n[sem] += 16
        return self.n[sem]

    def mm(self, out, lhsT, rhs, start, stop, inc=None):
        if self.cur == "tensor":
            i = self.e.matmul(out, lhsT, rhs, start=start, stop=stop)
            if inc:
                i.then_inc(self._sem(inc), 1)
        if inc:
            self.n[inc] += 1
            return self.n[inc]
        return None

    def tr(self, out, in_, inc=None):
        if self.cur == "tensor":
            i = self.e.transpose(out, in_, self.H["ident"])
            if inc:
                i.then_inc(self._sem(inc), 1)
        if inc:
            self.n[inc] += 1
            return self.n[inc]
        return None

    def act(self, out, in_, func, bias=0.0, scale=1.0):
        if self.cur == "scalar":
            self.e.activation(out, in_, func, bias=bias, scale=scale).then_inc(self._sem("AC"), 1)
        self.n["AC"] += 1
        return self.n["AC"]

    def dve(self, op, *args, **kw):
        if self.cur == "vector":
            getattr(self.e, op)(*args, **kw).then_inc(self._sem("DV"), 1)
        self.n["DV"] += 1
        return self.n["DV"]

    def coll(self, kind, in_ap, out_ap, sem):
        if self.cur == "gpsimd":
            self.e.collective_compute(
                kind, ALU.bypass,
                replica_groups=[[0, 1, 2, 3], [4, 5, 6, 7]],
                ins=[in_ap], outs=[out_ap],
            ).then_inc(self._sem(sem))

    def barrier(self, extra_cc=()):
        for s in ("MM", "AC", "DV", "MF"):
            if self.n[s]:
                self.e.wait_ge(self._sem(s), self.n[s])
        for s, eng in DMA_SEMS.items():
            if self.cur == eng and self.n[s]:
                self.e.wait_ge(self._sem(s), self.n[s])
        if self.cur == "gpsimd":
            for cc in extra_cc:
                self.e.wait_ge(self._sem(cc), 1)
        self.e.sem_inc(self._sem("PH"), 1)
        self.n["PH"] += 5
        self.e.wait_ge(self._sem("PH"), self.n["PH"])

    def ln_stats(self, x_dram):
        H = self.H
        pb = H["pb"]
        xr, sq, xb = H["xr4"], H["sq"], H["xb"]
        ld = [0] * 32
        acm = [0] * 32
        dvm = [0] * 32
        mmg = [0] * 32
        for c in range(32):
            if c >= 4:
                self.wait(SY, "AC", acm[c - 4])
                self.wait(SY, "DV", dvm[c - 4])
            ld[c] = self.dma(SY, xr[c % 4], x_dram[128 * c:128 * (c + 1), :], f"XR{c % 4}")
            self.wait("scalar", f"XR{c % 4}", ld[c])
            if c >= 2:
                self.wait("scalar", "MM", mmg[c - 2])
            acm[c] = self.act(sq[c % 2], xr[c % 4], AF.Square)
            self.wait("vector", f"XR{c % 4}", ld[c])
            if c >= 2:
                self.wait("vector", "MM", mmg[c - 2])
            dvm[c] = self.dve("tensor_copy", xb[c % 2], xr[c % 4])
            self.wait("tensor", "DV", dvm[c])
            self.wait("tensor", "AC", acm[c])
            for tt in range(2):
                sl = slice(512 * tt, 512 * (tt + 1))
                self.mm(pb[tt][0:1, :], H["ones_col"], xb[c % 2][:, sl],
                        start=(c == 0), stop=(c == 31), inc="MM")
                mmg[c] = self.mm(pb[2 + tt][0:1, :], H["ones_col"], sq[c % 2][:, sl],
                                 start=(c == 0), stop=(c == 31), inc="MM")
        rm, rv, ra, rb = H["row_m"], H["row_v"], H["row_a"], H["row_b"]
        self.wait("scalar", "MM", mmg[31])
        a_rows = 0
        for tt in range(2):
            sl = slice(512 * tt, 512 * (tt + 1))
            self.act(rm[0:1, sl], pb[tt][0:1, :], AF.Copy, scale=1.0 / 4096.0)
            a_rows = self.act(rv[0:1, sl], pb[2 + tt][0:1, :], AF.Copy, scale=1.0 / 4096.0)
        self.wait("vector", "AC", a_rows)
        d_msq = self.dve("tensor_mul", rb[0:1, :], rm[0:1, :], rm[0:1, :])
        self.wait("vector", "DV", d_msq)
        d_veps = self.dve("scalar_tensor_tensor", ra[0:1, :], rv[0:1, :], EPS, rb[0:1, :],
                          ALU.add, ALU.subtract)
        self.wait("scalar", "DV", d_veps)
        a_ln = self.act(rb[0:1, :], ra[0:1, :], AF.Ln)
        self.wait("scalar", "AC", a_ln)
        a_rstd = self.act(ra[0:1, :], rb[0:1, :], AF.Exp, scale=-0.5)
        self.wait("vector", "AC", a_rstd)
        d_b = self.dve("scalar_tensor_tensor", rb[0:1, :], rm[0:1, :], -1.0, ra[0:1, :],
                       ALU.mult, ALU.mult)
        self.wait("tensor", "DV", d_b)
        for tt in range(2):
            sl = slice(512 * tt, 512 * (tt + 1))
            self.mm(pb[4 + tt][:, :], H["ones_row"][0:1, :], ra[0:1, sl],
                    start=True, stop=True, inc="MM")
            self.mm(pb[6 + tt][:, :], H["ones_row"][0:1, :], rb[0:1, sl],
                    start=True, stop=True, inc="MM")
        return self.n["MM"]

    def ln_apply(self, x_dram, bcast_mm):
        H = self.H
        pb = H["pb"]
        xr, tmp, xlnv = H["xr4"], H["dtmp"], H["xln_v"]
        dvm = [0] * 32
        for c in range(32):
            if c >= 4:
                self.wait(SY, "DV", dvm[c - 4])
            ld = self.dma(SY, xr[c % 4], x_dram[128 * c:128 * (c + 1), :], f"XR{c % 4}")
            self.wait("vector", f"XR{c % 4}", ld)
            if c == 0:
                self.wait("vector", "MM", bcast_mm)
            for tt in range(2):
                sl = slice(512 * tt, 512 * (tt + 1))
                d_ = self.dve("tensor_mul", tmp[tt], xr[c % 4][:, sl], pb[4 + tt][:, :])
                self.wait("vector", "DV", d_)
                dvm[c] = self.dve("tensor_add", xlnv[:, c, sl], tmp[tt], pb[6 + tt][:, :])

    def ffn_a(self, app):
        H = self.H
        pb = H["pb"]
        xlnv = H["xln_v"]
        self.dma(GP, H["a1sb"], H["a1_d"], "AL")
        self.dma(GP, H["a3sb"], H["a3_d"], "AL")
        al = self.dma(GP, H["a2sb"], H["a2_d"], "AL")
        self.wait("tensor", "AL", al)
        for tt in range(2):
            sl = slice(512 * tt, 512 * (tt + 1))
            for c in range(32):
                self.mm(pb[0 + tt][0:16, :], H["a1sb"][:, c, :], xlnv[:, c, sl],
                        start=(c == 0), stop=(c == 31), inc="MM" if c == 31 else None)
                self.mm(pb[2 + tt][0:16, :], H["a3sb"][:, c, :], xlnv[:, c, sl],
                        start=(c == 0), stop=(c == 31), inc="MM" if c == 31 else None)
        z_mm = self.n["MM"]
        self.wait("scalar", "MM", z_mm)
        z_ac = 0
        for tt in range(2):
            sl = slice(512 * tt, 512 * (tt + 1))
            self.act(H["z1sb"][0:16, sl], pb[0 + tt][0:16, :], AF.Copy, scale=2.0)
            z_ac = self.act(H["z3sb"][0:16, sl], pb[2 + tt][0:16, :], AF.Copy, scale=2.0)

        w1sb, w3sb, gsb = H["w1sb"], H["w3sb"], H["gsb"]
        b1r, b3r = H["b1r"], H["b3r"]
        wa = [0] * NHC
        ga = [0] * NHC
        grp_h1 = [[0, 0] for _ in range(NHC)]
        grp_h3 = [[0, 0] for _ in range(NHC)]
        grp4 = [0] * NHC
        silu_m = [[0, 0] for _ in range(NHC)]
        dvg = [0] * NHC

        for h in range(NHC):
            p = h % 2
            if h >= 2:
                self.wait(GP, "MM", grp4[h - 2])
                if ga[h - 2]:
                    self.wait(GP, f"GA{h % 2}", ga[h - 2])
            if app == 1:
                self.dma(GP, w1sb[p], H["w1t_d"][h], f"WA{p}")
                self.dma(GP, w3sb[p], H["w3t_d"][h], f"WA{p}")
            else:
                self.dma(GP, w1sb[p], H["w1b_d"][h], f"WA{p}")
                self.dma(GP, w3sb[p], H["w3b_d"][h], f"WA{p}")
            self.dma(GP, b1r[p][0:16, :], H["b1_d"][h], f"WA{p}")
            wa[h] = self.dma(GP, b3r[p][0:16, :], H["b3_d"][h], f"WA{p}")
            if h >= 1:
                q = (h - 1) % 2
                if app == 1:
                    self.wait(GP, f"WA{q}", wa[h - 1])
                    self.dma(GP, H["w1b_d"][h - 1], w1sb[q], f"GA{q}")
                    self.dma(GP, H["w3b_d"][h - 1], w3sb[q], f"GA{q}")
                self.wait(GP, "DV", dvg[h - 1])
                ga[h - 1] = self.dma(GP, H["g_d"][h - 1], gsb[q], f"GA{q}")

            if h >= 1:
                self.wait("tensor", "DV", dvg[h - 1])
                for tt in range(2):
                    self.mm(pb[6 + tt][0:16, :], H["a2sb"][:, h - 1, :],
                            gsb[(h - 1) % 2][:, 512 * tt:512 * (tt + 1)],
                            start=(h - 1 == 0), stop=False)
            self.wait("tensor", f"WA{p}", wa[h])
            bank = [pb[(4 * h + i) % 6] for i in range(4)]
            if h == 0:
                self.wait("tensor", "AC", z_ac)
            if h >= 2:
                self.wait("tensor", "DV", dvg[h - 2])
            if h >= 1:
                self.wait("tensor", "AC", silu_m[h - 1][1])
            for tt in range(2):
                sl = slice(512 * tt, 512 * (tt + 1))
                for c in range(32):
                    self.mm(bank[tt][:, :], w1sb[p][:, c, :], xlnv[:, c, sl],
                            start=(c == 0), stop=False)
                grp_h1[h][tt] = self.mm(bank[tt][:, :], b1r[p][0:16, :],
                                        H["z1sb"][0:16, sl], start=False, stop=True, inc="MM")
            for tt in range(2):
                sl = slice(512 * tt, 512 * (tt + 1))
                for c in range(32):
                    self.mm(bank[2 + tt][:, :], w3sb[p][:, c, :], xlnv[:, c, sl],
                            start=(c == 0), stop=False)
                grp_h3[h][tt] = self.mm(bank[2 + tt][:, :], b3r[p][0:16, :],
                                        H["z3sb"][0:16, sl], start=False, stop=True, inc="MM")
            grp4[h] = grp_h3[h][1]

            for tt in range(2):
                self.wait("scalar", "MM", grp_h1[h][tt])
                if h >= 1:
                    self.wait("scalar", "DV", dvg[h - 1])
                silu_m[h][tt] = self.act(H["ssb"][tt], bank[tt][:, :], AF.Silu)

            for tt in range(2):
                self.wait("vector", "AC", silu_m[h][tt])
                self.wait("vector", "MM", grp_h3[h][tt])
                if h >= 2 and tt == 0:
                    self.wait("vector", "MM", grp4[h - 1])
                    self.wait("vector", f"GA{h % 2}", ga[h - 2])
                dvg[h] = self.dve("tensor_mul", gsb[p][:, 512 * tt:512 * (tt + 1)],
                                  H["ssb"][tt], bank[2 + tt][:, :])

        self.wait("tensor", "DV", dvg[NHC - 1])
        z2_mm = 0
        for tt in range(2):
            z2_mm = self.mm(pb[6 + tt][0:16, :], H["a2sb"][:, NHC - 1, :],
                            gsb[(NHC - 1) % 2][:, 512 * tt:512 * (tt + 1)],
                            start=False, stop=True, inc="MM")
        q = (NHC - 1) % 2
        if app == 1:
            self.wait(GP, f"WA{q}", wa[NHC - 1])
            self.dma(GP, H["w1b_d"][NHC - 1], w1sb[q], f"GA{q}")
            self.dma(GP, H["w3b_d"][NHC - 1], w3sb[q], f"GA{q}")
        self.wait(GP, "DV", dvg[NHC - 1])
        self.dma(GP, H["g_d"][NHC - 1], gsb[q], f"GA{q}")
        self.wait("scalar", "MM", z2_mm)
        for tt in range(2):
            self.act(H["z2sb"][0:16, 512 * tt:512 * (tt + 1)], pb[6 + tt][0:16, :],
                     AF.Copy, scale=2.0)

    def ffn_b(self, app, x_in_dram, x_out_dram):
        H = self.H
        pb = H["pb"]
        w2r, gt = H["w2r"], H["gtr"]
        xr, dtmp, xout = H["xr"], H["dtmp"], H["xout"]
        NI = NHC // 2
        wb = [0] * (8 * NI)
        gl = [0] * (8 * NI)
        mf = [0] * (8 * NI)
        bb = [0] * 8
        b2m = [0] * 8
        evac_ac = [[0] * 8 for _ in range(8)]
        dv_add = {}
        xo = {}
        pend = {}
        for dg in range(8):
            if dg >= 2:
                self.wait(GP, "MM", b2m[dg - 2])
            bb[dg] = self.dma(GP, H["b2r"][dg % 2][0:16, :, :],
                              H["b2_d"][:, 4 * dg:4 * (dg + 1), :], f"BB{dg % 2}")
            for hh in range(NI):
                it = dg * NI + hh
                p = it % 4
                if it >= 4:
                    self.wait(GP, "MF", mf[it - 4])
                wb[it] = self.dma(GP, w2r[p],
                                  H["w2t_d"][dg, 2 * hh:2 * hh + 2]
                                  .rearrange("h p j -> p h j"), f"WB{p}")
                if it >= 4:
                    self.wait(SY, "MF", mf[it - 4])
                gl[it] = self.dma(SY, gt[p],
                                  H["g_d"][2 * hh:2 * hh + 2]
                                  .rearrange("h p t -> p h t"), f"GL{p}")
                self.wait("tensor", f"WB{p}", wb[it])
                self.wait("tensor", f"GL{p}", gl[it])
                for hi in range(2):
                    if hh == 0 and hi == 0 and dg >= 1:
                        for k in range(4):
                            for tt in range(2):
                                self.wait("tensor", "AC", evac_ac[dg - 1][2 * k + tt])
                                self.mm(pb[2 * k + tt][:, :], w2r[p][:, hi, 128 * k:128 * (k + 1)],
                                        gt[p][:, hi, 512 * tt:512 * (tt + 1)],
                                        start=True, stop=False, inc=None)
                        continue
                    for k in range(4):
                        for tt in range(2):
                            r = self.mm(pb[2 * k + tt][:, :], w2r[p][:, hi, 128 * k:128 * (k + 1)],
                                        gt[p][:, hi, 512 * tt:512 * (tt + 1)],
                                        start=(hh == 0 and hi == 0), stop=False,
                                        inc="MF" if (hi == 1 and k == 3 and tt == 1) else None)
                            if r:
                                mf[it] = r
            self.wait("tensor", f"BB{dg % 2}", bb[dg])
            stop_m = [0] * 8
            for k in range(4):
                for tt in range(2):
                    stop_m[2 * k + tt] = self.mm(
                        pb[2 * k + tt][:, :], H["b2r"][dg % 2][0:16, k, :],
                        H["z2sb"][0:16, 512 * tt:512 * (tt + 1)],
                        start=False, stop=True, inc="MM")
            b2m[dg] = stop_m[7]
            for k in range(4):
                kg = 4 * dg + k
                c = kg
                if kg >= 2:
                    self.wait("scalar", "DV", dv_add.get((kg - 2, 1), 0))
                xrl = self.dma("scalar", xr[k % 2], x_in_dram[128 * c:128 * (c + 1), :],
                               f"XP{kg % 2}")
                for tt in range(2):
                    bi = 2 * k + tt
                    self.wait("scalar", "MM", stop_m[bi])
                    if (kg - 1, tt) in pend:
                        self.wait("scalar", "DV", dv_add[(kg - 1, tt)])
                        pc = kg - 1
                        xo[(kg - 1, tt)] = self.dma(
                            "scalar",
                            pend.pop((kg - 1, tt))[128 * pc:128 * (pc + 1),
                                                   512 * tt:512 * (tt + 1)],
                            xout[tt], f"XQ{tt}")
                    a_ = self.act(dtmp[tt], pb[bi][:, :], AF.Copy, scale=0.5)
                    evac_ac[dg][bi] = a_
                    self.wait("vector", "AC", a_)
                    self.wait("vector", f"XP{kg % 2}", xrl)
                    self.wait("vector", f"XQ{tt}", xo.get((kg - 1, tt), 0))
                    dv_add[(kg, tt)] = self.dve("tensor_add", xout[tt], dtmp[tt],
                                                xr[k % 2][:, 512 * tt:512 * (tt + 1)])
                    pend[(kg, tt)] = x_out_dram
        for tt in range(2):
            kg = 31
            self.wait("scalar", "DV", dv_add[(kg, tt)])
            xo[(kg, tt)] = self.dma(
                "scalar", pend.pop((kg, tt))[128 * kg:128 * (kg + 1),
                                             512 * tt:512 * (tt + 1)],
                xout[tt], f"XQ{tt}")
        assert not pend

    def qkv_rope(self, sections=("K", "V", "C", "Q")):
        H = self.H
        pb = H["pb"]
        xlnv = H["xln_v"]
        wsl = H["wslab"]
        cos, sin = H["cos"], H["sin"]

        def proj_group(o, tt, bank, slab):
            sl = slice(512 * tt, 512 * (tt + 1))
            m = None
            for c in range(32):
                m = self.mm(bank[:, :], slab[:, c, :], xlnv[:, c, sl],
                            start=(c == 0), stop=(c == 31),
                            inc="MM" if c == 31 else None)
            return m

        wq = [0] * 48
        pg = [[0, 0] for _ in range(48)]
        rope_dv = [[0, 0] for _ in range(48)]
        kbf_ac = [[0, 0] for _ in range(48)]
        kst_gs = [[0, 0] for _ in range(48)]
        vt_dv = [[0, 0] for _ in range(48)]
        vt_gs = [[0, 0] for _ in range(48)]
        tr_mm = [[0, 0] for _ in range(48)]

        for o in range(8 if "K" in sections else 0):
            p = o % 2
            if o >= 2:
                self.wait(GP, "MM", pg[o - 2][1])
            wq[o] = self.dma(GP, wsl[p], H["wkt_d"][o], f"WQ{p}")
            self.wait("tensor", f"WQ{p}", wq[o])
            for tt in range(2):
                sl = slice(512 * tt, 512 * (tt + 1))
                if o >= 1:
                    self.wait("tensor", "DV", rope_dv[o - 1][tt])
                pg[o][tt] = proj_group(o, tt, pb[tt], wsl[p])
                self.wait("scalar", "MM", pg[o][tt])
                kbf_ac[o][tt] = self.act(H["kbf"][tt], pb[tt][:, :], AF.Copy)
                self.wait("tensor", "AC", kbf_ac[o][tt])
                rot_m = self.mm(pb[2 + tt][:, :], H["prot"], H["kbf"][tt],
                                start=True, stop=True, inc="MM")
                self.wait("vector", "MM", rot_m)
                if o >= 1:
                    self.wait("vector", f"KS{tt}", kst_gs[o - 1][tt])
                self.wait("vector", "DV", self.n["DV"])
                self.dve("tensor_mul", H["t1"], pb[tt][:, :], cos[:, sl])
                d_ = self.dve("tensor_mul", H["t2"], pb[2 + tt][:, :], sin[:, sl])
                self.wait("vector", "DV", d_)
                rope_dv[o][tt] = self.dve("tensor_add", H["kst"][tt], H["t1"], H["t2"])
                self.wait(GP, "DV", rope_dv[o][tt])
                kst_gs[o][tt] = self.dma(GP, H["ksend_d"][128 * o:128 * (o + 1), sl],
                                         H["kst"][tt], f"KS{tt}")

        for oo in range(8 if "V" in sections else 0):
            o = 8 + oo
            p = o % 2
            if oo >= 2:
                self.wait(GP, "MM", pg[o - 2][1])
            wq[o] = self.dma(GP, wsl[p], H["wvt_d"][oo], f"WQ{p}")
            self.wait("tensor", f"WQ{p}", wq[o])
            for tt in range(2):
                if oo >= 1:
                    self.wait("tensor", "AC", kbf_ac[o - 1][tt])
                pg[o][tt] = proj_group(o, tt, pb[tt], wsl[p])
                self.wait("scalar", "MM", pg[o][tt])
                if oo >= 1:
                    self.wait("scalar", "MM", tr_mm[o - 1][tt])
                kbf_ac[o][tt] = self.act(H["vf32"][tt], pb[tt][:, :], AF.Copy)
                self.wait("tensor", "AC", kbf_ac[o][tt])
                prev = vt_dv[o][0] if tt == 1 else (vt_dv[o - 1][1] if oo >= 1 else 0)
                self.wait("tensor", "DV", prev)
                for tau in range(4):
                    tr_mm[o][tt] = self.tr(pb[4][:, 128 * tau:128 * (tau + 1)],
                                           H["vf32"][tt][:, 128 * tau:128 * (tau + 1)],
                                           inc="MM" if tau == 3 else None)
                self.wait("vector", "MM", tr_mm[o][tt])
                if oo >= 1:
                    self.wait("vector", f"VS{tt}", vt_gs[o - 1][tt])
                vt_dv[o][tt] = self.dve("tensor_copy", H["vt"][tt], pb[4][:, :])
                self.wait(GP, "DV", vt_dv[o][tt])
                for tau in range(4):
                    vt_gs[o][tt] = self.dma(
                        GP, H["vtsend_d"][512 * tt + 128 * tau:512 * tt + 128 * (tau + 1),
                                          128 * oo:128 * (oo + 1)],
                        H["vt"][tt][:, 128 * tau:128 * (tau + 1)], f"VS{tt}")

        if "C" in sections and not self.H.get("sim_nocoll"):
            for s in ("KS0", "KS1", "VS0", "VS1"):
                self.wait(GP, s, self.n[s])
            self.coll("AllGather", H["ksend_d"], H["kgath_d"], "CC1")
            self.coll("AllGather", H["vtsend_d"], H["vtgath_d"], "CC2")

        if "Q" in sections:
            kbfq = [[0, 0] for _ in range(32)]
            sqq_ac = [[0, 0] for _ in range(32)]
            rope_mm = [[0, 0] for _ in range(32)]
            q_r2_mm = [[0, 0] for _ in range(32)]
            rt_dv = [0] * 32
            qrope_dv = [[0, 0] for _ in range(32)]

            def q_rot(oo):
                for tt in range(2):
                    self.wait("tensor", "AC", kbfq[oo][tt])
                    if oo >= 1:
                        self.wait("tensor", "DV", qrope_dv[oo - 1][1])
                    elif "V" in sections:
                        self.wait("tensor", "DV", vt_dv[15][1])
                    rope_mm[oo][tt] = self.mm(pb[4 + tt][:, :], H["prot"], H["kbf"][tt],
                                              start=True, stop=True, inc="MM")

            def q_r2(oo):
                for tt in range(2):
                    self.wait("tensor", "AC", sqq_ac[oo][tt])
                    if oo >= 1:
                        self.wait("tensor", "DV", rt_dv[oo - 1])
                    q_r2_mm[oo][tt] = self.mm(pb[6 + tt][0:1, :], H["ones_col"],
                                              H["ssb"][tt], start=True, stop=True, inc="MM")

            def q_rope_dve(oo):
                for tt in range(2):
                    sl = slice(512 * tt, 512 * (tt + 1))
                    self.wait("vector", "MM", rope_mm[oo][tt])
                    self.wait("vector", "DV", self.n["DV"])
                    self.dve("tensor_mul", H["t1"], pb[2 * (oo % 2) + tt][:, :], cos[:, sl])
                    d_ = self.dve("tensor_mul", H["t2"], pb[4 + tt][:, :], sin[:, sl])
                    self.wait("vector", "DV", d_)
                    qrope_dv[oo][tt] = self.dve("tensor_add", H["qp_v"][:, oo, sl],
                                                H["t1"], H["t2"])

            def q_sqq(oo):
                for tt in range(2):
                    self.wait("scalar", "DV", qrope_dv[oo][tt])
                    if oo >= 1:
                        self.wait("scalar", "MM", q_r2_mm[oo - 1][tt])
                    sqq_ac[oo][tt] = self.act(H["ssb"][tt],
                                              H["qp_v"][:, oo, 512 * tt:512 * (tt + 1)],
                                              AF.Square)

            def q_rstat(oo):
                self.wait("vector", "DV", self.n["DV"])
                for tt in range(2):
                    self.wait("vector", "MM", q_r2_mm[oo][tt])
                    self.dve("tensor_reduce", H["rtmp"][0:1, tt:tt + 1],
                             pb[6 + tt][0:1, :], AX.X, ALU.max)
                rt_dv[oo] = self.n["DV"]
                self.wait("vector", "DV", rt_dv[oo])
                self.dve("tensor_max", H["rmax"][0:1, oo:oo + 1], H["rtmp"][0:1, 0:1],
                         H["rtmp"][0:1, 1:2])

            for oo in range(32):
                o = 16 + oo
                p = o % 2
                if oo >= 2:
                    self.wait(GP, "MM", pg[o - 2][1])
                wq[o] = self.dma(GP, wsl[p], H["wqt_d"][oo], f"WQ{p}")
                self.wait("tensor", f"WQ{p}", wq[o])
                for tt in range(2):
                    if oo >= 2:
                        self.wait("tensor", "DV", qrope_dv[oo - 2][tt])
                    elif "V" in sections:
                        self.wait("tensor", "AC", kbf_ac[15][1])
                    pg[o][tt] = proj_group(o, tt, pb[2 * (oo % 2) + tt], wsl[p])
                if oo >= 1:
                    q_rot(oo - 1)
                if oo >= 2:
                    q_r2(oo - 2)
                for tt in range(2):
                    self.wait("scalar", "MM", pg[o][tt])
                    if oo >= 1:
                        self.wait("scalar", "MM", rope_mm[oo - 1][tt])
                    kbfq[oo][tt] = self.act(H["kbf"][tt], pb[2 * (oo % 2) + tt][:, :],
                                            AF.Copy)
                if oo >= 1:
                    q_rope_dve(oo - 1)
                    q_sqq(oo - 1)
                if oo >= 2:
                    q_rstat(oo - 2)
            q_rot(31)
            q_r2(30)
            q_rope_dve(31)
            q_sqq(31)
            q_rstat(30)
            q_r2(31)
            q_rstat(31)

    def attention(self):
        H = self.H
        pb = H["pb"]
        qp = H["qp_v"]
        mend = [0] * 8
        dend = [0] * 8
        aend = [0] * 8
        for hb in range(8):
            par = hb % 2
            ksb, vtsb, vts = H["ksb"][par], H["vtsb"][par], H["vts"][par]
            if hb == 0 and not H.get("sim_nocoll"):
                self.wait(SY, "CC1", 1)
                self.wait(SY, "CC2", 1)
            if hb >= 2:
                self.wait(SY, "MM", mend[hb - 2])
                self.wait(SY, "DV", dend[hb - 2])
                self.wait(SY, "AC", aend[hb - 2])
            kvm = 0
            for c in range(8):
                r, half = min(c, 7 - c), (0 if c <= 3 else 1)
                kvm = self.dma(SY, ksb[:, 512 * c:512 * (c + 1)],
                               H["kgath_d"][1024 * r + 128 * hb:1024 * r + 128 * (hb + 1),
                                            512 * half:512 * (half + 1)], f"KV{par}")
            for c in range(8):
                r, half = min(c, 7 - c), (0 if c <= 3 else 1)
                src = H["vtgath_d"][1024 * r + 512 * half:1024 * r + 512 * (half + 1),
                                    128 * hb:128 * (hb + 1)]
                kvm = self.dma(SY, vtsb[:, 4 * c:4 * (c + 1), :],
                               src.rearrange("(s p) j -> p s j", p=128), f"KV{par}")
            self.wait("vector", f"KV{par}", kvm)
            self.wait("scalar", f"KV{par}", kvm)
            if hb >= 1:
                self.wait("scalar", "MM", mend[hb - 1])
                self.wait("scalar", "DV", dend[hb - 1])
            if hb >= 1:
                self.wait("vector", "MM", mend[hb - 1])
                self.wait("vector", "AC", aend[hb - 1])
            sqk_dv = 0
            for i in range(8):
                sqk_dv = self.dve("tensor_mul", H["sqk"][:, 512 * i:512 * (i + 1)],
                                  ksb[:, 512 * i:512 * (i + 1)],
                                  ksb[:, 512 * i:512 * (i + 1)])
            self.wait("tensor", "DV", sqk_dv)
            r2k_mm = None
            for m in range(32):
                r2k_mm = self.mm(pb[5][:, m:m + 1], H["sqk"][:, 128 * m:128 * (m + 1)],
                                 H["ones_col"], start=True, stop=True,
                                 inc="MM" if m == 31 else None)
            qm_dv = self.dve("tensor_reduce", H["qmaxr"][0:1, 0:1],
                             H["rmax"][0:1, 4 * hb:4 * (hb + 1)], AX.X, ALU.max)
            self.wait("scalar", "DV", qm_dv)
            qs_ac = self.act(H["qsr"][0:1, 0:1], H["qmaxr"][0:1, 0:1], AF.Sqrt)
            self.wait("tensor", "AC", qs_ac)
            qb_mm = self.mm(pb[4][:, 0:1], H["ones_row"][0:1, :], H["qsr"][0:1, 0:1],
                            start=True, stop=True, inc="MM")
            self.wait("scalar", "MM", r2k_mm)
            kabs_ac = self.act(H["kabs"][:, 0:32], pb[5][:, 0:32], AF.Sqrt)
            self.wait("vector", "MM", qb_mm)
            d_qb = self.dve("tensor_copy", H["qbcol"][:, 0:1], pb[4][:, 0:1])
            self.wait("vector", "AC", kabs_ac)
            self.wait("vector", "DV", d_qb)
            negc_dv = self.dve("tensor_scalar", H["negc"][:, 0:32], H["kabs"][:, 0:32],
                               H["qbcol"][:, 0:1], -S128, ALU.mult, ALU.mult)
            self.wait("scalar", "DV", negc_dv)
            self.act(H["expc"][:, 0:32], H["negc"][:, 0:32], AF.Exp, scale=-1.0)
            expc_ac = self.act(H["expcf"][:, 0:32], H["negc"][:, 0:32], AF.Exp, scale=-1.0)
            self.wait("vector", "AC", expc_ac)
            self.wait("vector", f"KV{par}", kvm)
            vts_dv = 0
            for m in range(32):
                vts_dv = self.dve("tensor_scalar", vts[:, m, :], vtsb[:, m, :],
                                  H["expcf"][:, m:m + 1], None, ALU.mult)
            prev_final_dv = vts_dv
            prev2_final_dv = vts_dv
            prev_lb_ac = expc_ac
            prev2_lb_ac = expc_ac
            prev_rec_dv = vts_dv
            prev2_rec_dv = vts_dv
            prev_pair = None
            T_prev = 0
            pi = 0
            for xi, T in ((0, TA), (1, TB)):
                for hq in range(4):
                    q = 4 * hb + hq
                    qsl = slice(512 * xi, 512 * (xi + 1))
                    exp_ac = [0] * T
                    stt_dv = [0] * T
                    qk_mm = [0] * T
                    avl_mm = [0] * T
                    sbank = (pb[0], pb[1], pb[5])
                    ob = pb[2] if pi % 2 == 0 else pb[6]
                    lbk = pb[3] if pi % 2 == 0 else pb[7]
                    def emit_qk(m):
                        if m < 3 and prev_pair is not None:
                            self.wait("tensor", "AC", prev_pair[0][-1])
                        elif m < 3 and prev_pair is None:
                            self.wait("tensor", "AC", prev_lb_ac)
                        qk_mm[m] = self.mm(sbank[m % 3][:, :], ksb[:, 128 * m:128 * (m + 1)],
                                           qp[:, q, qsl], start=True, stop=True, inc="MM")

                    def emit_exp(m):
                        self.wait("scalar", "MM", qk_mm[m])
                        if m >= 3:
                            self.wait("scalar", "DV", stt_dv[m - 3])
                        elif prev_pair is not None:
                            self.wait("scalar", "DV", prev_pair[1][-1])
                        exp_ac[m] = self.act(H["p_sb"][m % 3], sbank[m % 3][:, :], AF.Exp,
                                             bias=H["negc"][:, m:m + 1], scale=S128)

                    def emit_stt(m):
                        self.wait("vector", "AC", exp_ac[m])
                        if m >= 3:
                            self.wait("vector", "MM", avl_mm[m - 3])
                        elif prev_pair is not None:
                            self.wait("vector", "MM", prev_pair[2][-1])
                        stt_dv[m] = self.dve("scalar_tensor_tensor", H["pm_sb"][m % 3],
                                             H["iota"], H["thr"][:, 16 * xi + m:16 * xi + m + 1],
                                             H["p_sb"][m % 3], ALU.is_ge, ALU.mult)

                    def emit_avl(m):
                        self.wait("tensor", "DV", stt_dv[m])
                        if m == 0:
                            self.wait("tensor", "DV", prev2_final_dv)
                            self.wait("tensor", "DV", prev2_rec_dv)
                        self.mm(ob[:, :], vts[:, m, :], H["pm_sb"][m % 3],
                                start=(m == 0), stop=(m == T - 1), inc=None)
                        avl_mm[m] = self.mm(lbk[0:1, :], H["expc"][:, m:m + 1],
                                            H["pm_sb"][m % 3],
                                            start=(m == 0), stop=(m == T - 1), inc="MM")

                    for m0 in range(3):
                        emit_qk(m0)
                        emit_exp(m0)
                        emit_stt(m0)
                    for m in range(T):
                        emit_avl(m)
                        if m + 3 < T:
                            emit_qk(m + 3)
                            emit_exp(m + 3)
                            emit_stt(m + 3)
                    self.wait("vector", "MM", avl_mm[T - 1])
                    self.wait("vector", "MM", mend[hb])
                    rec_dv = self.dve("reciprocal", H["linv"][0:1, :], lbk[0:1, :])
                    self.wait("tensor", "DV", rec_dv)
                    self.wait("tensor", "AC", prev_lb_ac)
                    lb_mm = self.mm(pb[4][:, :], H["ones_row"][0:1, :], H["linv"][0:1, :],
                                    start=True, stop=True, inc="MM")
                    self.wait("scalar", "MM", lb_mm)
                    lb_ac = self.act(H["lb_sb"], pb[4][:, :], AF.Copy)
                    self.wait("vector", "AC", lb_ac)
                    fin_dv = self.dve("tensor_mul", qp[:, q, qsl], ob[:, :], H["lb_sb"])
                    prev2_final_dv, prev_final_dv = prev_final_dv, fin_dv
                    prev2_rec_dv, prev_rec_dv = prev_rec_dv, rec_dv
                    prev2_lb_ac, prev_lb_ac = prev_lb_ac, lb_ac
                    prev_pair = (exp_ac, stt_dv, avl_mm)
                    T_prev = T
                    pi += 1
                    mend[hb] = lb_mm
            dend[hb] = prev_final_dv
            aend[hb] = prev_lb_ac

    def wo_proj(self, x_in_dram, x_out_dram):
        H = self.H
        pb = H["pb"]
        qp = H["qp_v"]
        wsl = H["wosb"]
        xr, dtmp, xout = H["xr"], H["dtmp"], H["xout"]
        wq = [0] * 256
        mf = [0] * 256
        ac8 = [0] * 8
        dv_add = {}
        xo = {}
        for dg in range(8):
            stop_m = [0] * 8
            for o in range(32):
                it = 32 * dg + o
                p = it % 4
                if it >= 4:
                    self.wait(GP, "MF", mf[it - 4])
                wq[it] = self.dma(GP, wsl[p], H["wot_d"][dg, o], f"WQ{p}")
                self.wait("tensor", f"WQ{p}", wq[it])
                if o == 0 and dg >= 1:
                    self.wait("tensor", "AC", ac8[dg - 1])
                for k in range(4):
                    for tt in range(2):
                        inc = "MF" if (k == 3 and tt == 1) else ("MM" if o == 31 else None)
                        r = self.mm(pb[2 * k + tt][:, :], wsl[p][:, k, :],
                                    qp[:, o, 512 * tt:512 * (tt + 1)],
                                    start=(o == 0), stop=(o == 31), inc=inc)
                        if k == 3 and tt == 1:
                            mf[it] = r
                        elif o == 31:
                            stop_m[2 * k + tt] = r
            stop_m[7] = mf[32 * dg + 31]
            for k in range(4):
                kg = 4 * dg + k
                c = kg
                if kg >= 2:
                    self.wait(SY, "DV", dv_add.get((kg - 2, 1), 0))
                xrl = self.dma(SY, xr[k % 2], x_in_dram[128 * c:128 * (c + 1), :],
                               f"XR{kg % 2}")
                for tt in range(2):
                    bi = 2 * k + tt
                    if bi == 7:
                        self.wait("scalar", "MF", stop_m[7])
                    else:
                        self.wait("scalar", "MM", stop_m[bi])
                    self.wait("scalar", "DV", dv_add.get((kg - 1, tt), 0))
                    a_ = self.act(dtmp[tt], pb[bi][:, :], AF.Copy, scale=1.0)
                    self.wait("vector", "AC", a_)
                    self.wait("vector", f"XR{kg % 2}", xrl)
                    self.wait("vector", f"XO{tt}", xo.get((kg - 1, tt), 0))
                    dv_add[(kg, tt)] = self.dve("tensor_add", xout[tt], dtmp[tt],
                                                xr[k % 2][:, 512 * tt:512 * (tt + 1)])
                    self.wait(SY, "DV", dv_add[(kg, tt)])
                    xo[(kg, tt)] = self.dma(
                        SY, x_out_dram[128 * c:128 * (c + 1), 512 * tt:512 * (tt + 1)],
                        xout[tt], f"XO{tt}")
                    ac8[dg] = a_



def build_nc(upto="full", sim_nocoll=False):
    nc = bass.Bass("TRN2", target_bir_lowering=False)

    def din(name, shape):
        return nc.dram_tensor(name, shape, F32, kind="ExternalInput").ap()

    DI = {}
    DI["x0"] = din("x0", [4096, 1024])
    DI["w1t_d"] = din("w1t", [NHC, 128, 32, 128])
    DI["w3t_d"] = din("w3t", [NHC, 128, 32, 128])
    DI["w2t_d"] = din("w2t", [8, NHC, 128, 512])
    DI["wqt_d"] = din("wqt", [32, 128, 32, 128])
    DI["wkt_d"] = din("wkt", [8, 128, 32, 128])
    DI["wvt_d"] = din("wvt", [8, 128, 32, 128])
    DI["wot_d"] = din("wot", [8, 32, 128, 512])
    DI["a1_d"] = din("a1", [128, 32, 16])
    DI["a3_d"] = din("a3", [128, 32, 16])
    DI["a2_d"] = din("a2", [128, NHC, 16])
    DI["b1_d"] = din("b1", [NHC, 16, 128])
    DI["b3_d"] = din("b3", [NHC, 16, 128])
    DI["b2_d"] = din("b2", [16, 32, 128])
    DI["cos_d"] = din("cosd", [128, 1024])
    DI["sin_d"] = din("sind", [128, 1024])
    DI["iota_d"] = din("iotad", [128, 512])
    DI["thr_d"] = din("thrd", [128, 48])
    DI["prot_d"] = din("protd", [128, 128])
    DI["ident_d"] = din("identd", [128, 128])
    DI["onesc_d"] = din("onescd", [128, 1])
    DI["onesr_d"] = din("onesrd", [1, 128])

    y = nc.dram_tensor("y", [4096, 1024], F32, kind="ExternalOutput").ap()
    dbg = None
    if upto != "full":
        dbg = nc.dram_tensor("dbg", [4096, 1024], F32, kind="ExternalOutput").ap()

    x1_d = nc.dram_tensor("x1", [4096, 1024], F32).ap()
    x2_d = nc.dram_tensor("x2", [4096, 1024], F32).ap()
    g_d = nc.dram_tensor("g_s", [NHC, 128, 1024], BF16).ap()
    w1b_d = nc.dram_tensor("w1b", [NHC, 128, 32, 128], BF16).ap()
    w3b_d = nc.dram_tensor("w3b", [NHC, 128, 32, 128], BF16).ap()
    w2b_d = nc.dram_tensor("w2b", [8, NHC, 128, 512], BF16).ap()
    ksend_t = nc.dram_tensor("ksend", [1024, 1024], BF16)
    vtsend_t = nc.dram_tensor("vtsend", [1024, 1024], BF16)
    kgath_t = nc.dram_tensor("kgath", [4096, 1024], BF16)
    vtgath_t = nc.dram_tensor("vtgath", [4096, 1024], BF16)

    with ExitStack() as ctx:
        sb = lambda name, shape, dt: ctx.enter_context(nc.sbuf_tensor(name, shape, dt))
        xln_a = sb("xln_a", [128, 32768], BF16)
        big = sb("big", [128, 32768], BF16)
        aux = sb("aux", [128, 8192], BF16)
        f32a = sb("f32a", [128, 6144], F32)
        cos_t = sb("cos", [128, 1024], F32)
        sin_t = sb("sin", [128, 1024], F32)
        prot_t = sb("prot", [128, 128], BF16)
        ident_t = sb("ident", [128, 128], F32)
        onesc_t = sb("ones_col", [128, 1], BF16)
        onesr_t = sb("ones_row", [1, 128], F32)
        iota_t = sb("iota", [128, 512], F32)
        thr_t = sb("thr", [128, 48], F32)
        z1_t = sb("z1sb", [16, 1024], BF16)
        z3_t = sb("z3sb", [16, 1024], BF16)
        z2_t = sb("z2sb", [16, 1024], BF16)
        b1r_t = [sb(f"b1r{i}", [16, 128], BF16) for i in range(2)]
        b3r_t = [sb(f"b3r{i}", [16, 128], BF16) for i in range(2)]
        b2r_t = [sb(f"b2r{i}", [16, 4, 128], BF16) for i in range(2)]
        a1_t = sb("a1sb", [128, 32, 16], BF16)
        a3_t = sb("a3sb", [128, 32, 16], BF16)
        a2_t = sb("a2sb", [128, NHC, 16], BF16)
        ssb_t = [sb(f"ssb{i}", [128, 512], BF16) for i in range(2)]
        kbf_t = [sb(f"kbf{i}", [128, 512], BF16) for i in range(2)]
        kst_t = [sb(f"kst{i}", [128, 512], BF16) for i in range(2)]
        vt_t = [sb(f"vt{i}", [128, 512], BF16) for i in range(2)]
        p_t = [sb(f"p{i}", [128, 512], BF16) for i in range(2)]
        pm_t = [sb(f"pm{i}", [128, 512], BF16) for i in range(2)]
        kabs_t = sb("kabs", [128, 32], F32)
        negc_t = sb("negc", [128, 32], F32)
        expc_t = sb("expc", [128, 32], BF16)
        expcf_t = sb("expcf", [128, 32], F32)
        qbcol_t = sb("qbcol", [128, 1], F32)
        rmax_t = sb("rmax", [1, 32], F32)
        rtmp_t = sb("rtmp", [1, 2], F32)
        qmaxr_t = sb("qmaxr", [1, 1], F32)
        qsr_t = sb("qsr", [1, 1], F32)

        pb = [ctx.enter_context(nc.psum_tensor(f"pb{i}", [128, 512], F32)) for i in range(8)]
        semnames = list(DMA_SEMS) + COMP_SEMS + ["CC1", "CC2"]
        sems = {s: ctx.enter_context(nc.semaphore(s)) for s in semnames}

        H_simflag = sim_nocoll
        bigap = big.ap()
        xlnap = xln_a.ap()
        auxap = aux.ap()
        f32ap = f32a.ap()
        H = dict(
            sim_nocoll=H_simflag,
            sems=sems, pb=pb,
            xln_v=xlnap.rearrange("p (c t) -> p c t", t=1024),
            qp_v=bigap.rearrange("p (o t) -> p o t", t=1024),
            w1sb=[bigap[:, 4096 * i:4096 * (i + 1)].rearrange("p (c j) -> p c j", j=128)
                  for i in range(2)],
            w3sb=[bigap[:, 4096 * (2 + i):4096 * (3 + i)].rearrange("p (c j) -> p c j", j=128)
                  for i in range(2)],
            gsb=[bigap[:, 16384 + 1024 * i:16384 + 1024 * (i + 1)] for i in range(2)],
            w2r=[bigap[:, 1024 * i:1024 * (i + 1)].rearrange("p (h j) -> p h j", j=512)
                 for i in range(4)],
            gtr=[bigap[:, 4096 + 2048 * i:4096 + 2048 * (i + 1)]
                 .rearrange("p (h t) -> p h t", t=1024) for i in range(4)],
            w2sb=[bigap[:, 18432 + 512 * i:18432 + 512 * (i + 1)]
                  .rearrange("p (k j) -> p k j", j=128) for i in range(2)],
            sq=[bigap[:, 19456 + 1024 * i:19456 + 1024 * (i + 1)] for i in range(2)],
            xb=[bigap[:, 21504 + 1024 * i:21504 + 1024 * (i + 1)] for i in range(2)],
            wslab=[auxap[:, 4096 * i:4096 * (i + 1)].rearrange("p (c j) -> p c j", j=128)
                   for i in range(2)],
            wosb=[auxap[:, 512 * i:512 * (i + 1)].rearrange("p (k j) -> p k j", j=128)
                  for i in range(4)],
            ksb=[xlnap[:, 4096 * i:4096 * (i + 1)] for i in range(2)],
            vtsb=[xlnap[:, 8192 + 4096 * i:8192 + 4096 * (i + 1)]
                  .rearrange("p (m j) -> p m j", j=128) for i in range(2)],
            vts=[xlnap[:, 16384 + 4096 * i:16384 + 4096 * (i + 1)]
                 .rearrange("p (m j) -> p m j", j=128) for i in range(2)],
            sqk=xlnap[:, 24576:28672],
            xr=[f32ap[:, 1024 * i:1024 * (i + 1)] for i in range(2)],
            xr4=[f32ap[:, 1024 * i:1024 * (i + 1)] for i in range(4)],
            t1=f32ap[:, 0:512], t2=f32ap[:, 512:1024],
            vf32=[f32ap[:, 1024 + 512 * i:1024 + 512 * (i + 1)] for i in range(2)],
            row_m=f32ap[:, 2048:3072], row_v=f32ap[:, 3072:4096],
            row_a=f32ap[:, 4096:5120], row_b=f32ap[:, 5120:6144],
            dtmp=[f32ap[:, 4096 + 512 * i:4096 + 512 * (i + 1)] for i in range(2)],
            xout=[f32ap[:, 5120 + 512 * i:5120 + 512 * (i + 1)] for i in range(2)],
            lb_sb=f32ap[:, 4096:4608], linv=f32ap[0:1, 2048:2560],
            cos=cos_t.ap(), sin=sin_t.ap(), prot=prot_t.ap(), ident=ident_t.ap(),
            ones_col=onesc_t.ap(), ones_row=onesr_t.ap(),
            iota=iota_t.ap(), thr=thr_t.ap(),
            z1sb=z1_t.ap(), z3sb=z3_t.ap(), z2sb=z2_t.ap(),
            b1r=[x.ap() for x in b1r_t], b3r=[x.ap() for x in b3r_t],
            b2r=[x.ap() for x in b2r_t],
            a1sb=a1_t.ap(), a3sb=a3_t.ap(), a2sb=a2_t.ap(),
            ssb=[x.ap() for x in ssb_t], kbf=[x.ap() for x in kbf_t],
            kst=[x.ap() for x in kst_t], vt=[x.ap() for x in vt_t],
            p_sb=[p_t[0].ap(), p_t[1].ap(), kst_t[0].ap()],
            pm_sb=[pm_t[0].ap(), pm_t[1].ap(), kst_t[1].ap()],
            kabs=kabs_t.ap(), negc=negc_t.ap(),
            expc=expc_t.ap(), expcf=expcf_t.ap(), qbcol=qbcol_t.ap(),
            rmax=rmax_t.ap(), rtmp=rtmp_t.ap(), qmaxr=qmaxr_t.ap(), qsr=qsr_t.ap(),
            g_d=g_d, w1b_d=w1b_d, w3b_d=w3b_d, w2b_d=w2b_d,
            ksend_d=ksend_t.ap(), vtsend_d=vtsend_t.ap(),
            kgath_d=kgath_t.ap(), vtgath_d=vtgath_t.ap(),
            **DI,
        )

        def walk(cur, eng):
            w = Walk(nc, cur, eng, H)
            w.dma(SY, cos_t.ap(), H["cos_d"], "XL")
            w.dma(SY, sin_t.ap(), H["sin_d"], "XL")
            w.dma(SY, iota_t.ap(), H["iota_d"], "XL")
            w.dma(SY, thr_t.ap(), H["thr_d"], "XL")
            w.dma(SY, ident_t.ap(), H["ident_d"], "XL")
            w.dma(SY, onesr_t.ap(), H["onesr_d"], "XL")
            w.dma(GP, prot_t.ap(), H["prot_d"], "WL")
            w.dma(GP, onesc_t.ap(), H["onesc_d"], "WL")
            w.barrier()

            def cut(name, src_dram):
                if upto == name:
                    w.barrier()
                    for c in range(8):
                        w.dma(SY, dbg[512 * c:512 * (c + 1), :],
                              src_dram[512 * c:512 * (c + 1), :], "XS")
                    w.wait(SY, "XS", w.n["XS"])
                    return True
                return False

            def cut_sbuf(name, view3):
                if upto == name:
                    w.barrier()
                    for c in range(32):
                        d = w.dve("tensor_copy", H["xr"][c % 2], view3[:, c, :])
                        w.wait(SY, "DV", d)
                        w.dma(SY, dbg[128 * c:128 * (c + 1), :], H["xr"][c % 2], "XS")
                        w.barrier()
                    w.wait(SY, "XS", w.n["XS"])
                    return True
                return False

            bc = w.ln_stats(H["x0"]); w.barrier()
            w.ln_apply(H["x0"], bc); w.barrier()
            if cut_sbuf("ln1", H["xln_v"]):
                return
            w.ffn_a(1); w.barrier()
            if upto == "gdump":
                for hh in range(32):
                    w.dma(GP, dbg[128 * hh:128 * (hh + 1), :], H["g_d"][hh], "GS")
                w.wait(GP, "GS", w.n["GS"])
                return
            w.ffn_b(1, H["x0"], x1_d); w.barrier()
            if cut("x1", x1_d):
                return
            bc = w.ln_stats(x1_d); w.barrier()
            w.ln_apply(x1_d, bc); w.barrier()
            if cut_sbuf("ln2", H["xln_v"]):
                return
            if upto in ("p7k", "p7v", "p7c", "p7q"):
                secs = {"p7k": ("K",), "p7v": ("K", "V"), "p7c": ("K", "V", "C"),
                        "p7q": ("K", "V", "Q")}[upto]
                w.qkv_rope(sections=secs)
                w.barrier(extra_cc=(("CC1", "CC2") if "C" in secs else ()))
                cut(upto, x1_d)
                return
            w.qkv_rope()
            w.barrier(extra_cc=(() if H.get("sim_nocoll") else ("CC1", "CC2")))
            if cut_sbuf("qkv", H["qp_v"]):
                return
            w.attention(); w.barrier()
            if cut_sbuf("attn", H["qp_v"]):
                return
            w.wo_proj(x1_d, x2_d); w.barrier()
            if cut("x2", x2_d):
                return
            bc = w.ln_stats(x2_d); w.barrier()
            w.ln_apply(x2_d, bc); w.barrier()
            if cut_sbuf("ln3", H["xln_v"]):
                return
            w.ffn_a(2); w.barrier()
            w.ffn_b(2, x2_d, y); w.barrier()
            for s in ("XO0", "XO1"):
                w.wait(SY, s, w.n[s])

        block = ctx.enter_context(nc.Block())
        block.gpsimd(lambda g: walk("gpsimd", g))
        block.sync(lambda s: walk("sync", s))
        block.tensor(lambda t: walk("tensor", t))
        block.scalar(lambda s: walk("scalar", s))
        block.vector(lambda v: walk("vector", v))

    return nc



def _prep_shared(inp):
    f32 = np.float32
    a = lambda k: np.asarray(inp[k], f32)
    W1, W3, W2 = a("L0_ffn_w1"), a("L0_ffn_w3"), a("L0_ffn_w2")
    Wq, Wk, Wv, Wo = a("L0_attn_Wq"), a("L0_attn_Wk"), a("L0_attn_Wv"), a("L0_attn_Wo")
    A1, A3, A2 = a("L0_ffn_lora_A1"), a("L0_ffn_lora_A3"), a("L0_ffn_lora_A2")
    B1, B3, B2 = a("L0_ffn_lora_B1"), a("L0_ffn_lora_B3"), a("L0_ffn_lora_B2")
    C = np.ascontiguousarray
    d = {}
    d["w1t"] = C(W1.reshape(NHC, 128, 32, 128).transpose(0, 3, 2, 1))
    d["w3t"] = C(W3.reshape(NHC, 128, 32, 128).transpose(0, 3, 2, 1))
    d["w2t"] = C(W2.reshape(8, 512, NHC, 128).transpose(0, 2, 3, 1))
    d["wqt"] = C(Wq.reshape(32, 128, 32, 128).transpose(0, 3, 2, 1))
    d["wkt"] = C(Wk.reshape(8, 128, 32, 128).transpose(0, 3, 2, 1))
    d["wvt"] = C(Wv.reshape(8, 128, 32, 128).transpose(0, 3, 2, 1))
    d["wot"] = C(Wo.reshape(8, 512, 32, 128).transpose(0, 2, 3, 1))
    d["a1"] = C(A1.reshape(16, 32, 128).transpose(2, 1, 0))
    d["a3"] = C(A3.reshape(16, 32, 128).transpose(2, 1, 0))
    d["a2"] = C(A2.reshape(16, NHC, 128).transpose(2, 1, 0))
    d["b1"] = C(B1.reshape(NHC, 128, 16).transpose(0, 2, 1))
    d["b3"] = C(B3.reshape(NHC, 128, 16).transpose(0, 2, 1))
    d["b2"] = C(B2.reshape(32, 128, 16).transpose(2, 0, 1))
    p = np.zeros((128, 128), f32)
    for m in range(64):
        p[m + 64, m] = -1.0
        p[m, m + 64] = 1.0
    d["protd"] = p
    d["identd"] = np.eye(128, dtype=f32)
    d["onescd"] = np.ones((128, 1), f32)
    d["onesrd"] = np.ones((1, 128), f32)
    d["iotad"] = np.broadcast_to(np.arange(512, dtype=f32), (128, 512)).copy()
    return d


def _prep_core(inp, core):
    f32 = np.float32
    b, j = core // 4, core % 4
    cA, cB = j, 7 - j
    x0 = np.asarray(inp["L0_attn_hidden_states"], f32)
    xc = np.ascontiguousarray(
        np.concatenate([x0[b, 512 * cA:512 * (cA + 1), :],
                        x0[b, 512 * cB:512 * (cB + 1), :]], 0).T)
    pos = np.concatenate([np.arange(512 * cA, 512 * (cA + 1)),
                          np.arange(512 * cB, 512 * (cB + 1))]).astype(f32)
    invf = (1.0 / (10000.0 ** (np.arange(0, 128, 2, dtype=f32) / f32(128.0)))).astype(f32)
    freqs = np.concatenate([invf, invf])[:, None] * pos[None, :]
    d = {"x0": xc,
         "cosd": np.cos(freqs).astype(f32),
         "sind": np.sin(freqs).astype(f32)}
    thr = np.full((128, 48), 1e9, f32)
    karr = np.arange(128, dtype=f32)
    for xi, (qb_, T) in enumerate(((512 * cA, TA), (512 * cB, TB))):
        for m in range(T):
            thr[:, 16 * xi + m] = 128 * m + karr - qb_
    d["thrd"] = thr
    return d


_BUILT = {}


def _get_nc(upto):
    if upto not in _BUILT:
        _BUILT[upto] = build_nc(upto)
    return _BUILT[upto]


def run(inputs, upto="full"):
    shared = _prep_shared(inputs)
    in_maps = []
    for core in range(8):
        m = dict(shared)
        m.update(_prep_core(inputs, core))
        in_maps.append(m)
    return run_bass_kernel_spmd(_get_nc(upto), in_maps, core_ids=list(range(8)))


def assemble(results, key="y"):
    out = np.zeros((2, 4096, 4096), np.float32)
    for core in range(8):
        b, j = core // 4, core % 4
        cA, cB = j, 7 - j
        yc = results[core][key]
        out[b, 512 * cA:512 * (cA + 1), :] = yc[:, 0:512].T
        out[b, 512 * cB:512 * (cB + 1), :] = yc[:, 512:1024].T
    return out


def kernel(**inputs):
    res = run(inputs, upto="full")
    return assemble(res.results, "y")


# version 06
# speedup vs baseline: 1.0464x; 1.0464x over previous
import numpy as np
import ml_dtypes
from contextlib import ExitStack

import concourse.bass as bass
import concourse.mybir as mybir
from concourse.bass_utils import run_bass_kernel_spmd

F32 = mybir.dt.float32
BF16 = mybir.dt.bfloat16
AF = mybir.ActivationFunctionType
ALU = mybir.AluOpType
AX = mybir.AxisListType

S128 = 0.08838834764831845
NHC = 86
TA, TB = 16, 32
EPS = 1e-5

GP = "gpsimd"
SY = "sync"

DMA_SEMS = {
    "WL": GP, "GS": GP, "AL": GP,
    "WA0": GP, "WA1": GP, "GA0": GP, "GA1": GP,
    "WB0": GP, "WB1": GP, "WB2": GP, "WB3": GP, "BB0": GP, "BB1": GP,
    "WS0": GP, "WS1": GP,
    "WQ0": GP, "WQ1": GP, "WQ2": GP, "WQ3": GP, "KS0": GP, "KS1": GP,
    "VS0": GP, "VS1": GP,
    "XR0": SY, "XR1": SY, "XR2": SY, "XR3": SY, "GL0": SY, "GL1": SY, "GL2": SY, "GL3": SY, "KV0": SY, "KV1": SY,
    "XO0": SY, "XO1": SY, "XL": SY, "XS": SY,
    "XP0": "scalar", "XP1": "scalar", "XQ0": "scalar", "XQ1": "scalar",
}
COMP_SEMS = ["MM", "AC", "DV", "MF", "PH"]


class Walk:
    def __init__(self, nc, cur, eng, H):
        self.nc = nc
        self.cur = cur
        self.e = eng
        self.H = H
        self.n = {s: 0 for s in list(DMA_SEMS) + COMP_SEMS}

    def _sem(self, name):
        return self.H["sems"][name]

    def wait(self, eng, sem, val):
        if val is None or val <= 0:
            return
        if self.cur == eng:
            self.e.wait_ge(self._sem(sem), val)

    def dma(self, eng, out, in_, sem):
        assert DMA_SEMS[sem] == eng, (sem, eng)
        if self.cur == eng:
            self.e.dma_start(out=out, in_=in_).then_inc(self._sem(sem), 16)
        self.n[sem] += 16
        return self.n[sem]

    def mm(self, out, lhsT, rhs, start, stop, inc=None):
        if self.cur == "tensor":
            i = self.e.matmul(out, lhsT, rhs, start=start, stop=stop)
            if inc:
                i.then_inc(self._sem(inc), 1)
        if inc:
            self.n[inc] += 1
            return self.n[inc]
        return None

    def tr(self, out, in_, inc=None):
        if self.cur == "tensor":
            i = self.e.transpose(out, in_, self.H["ident"])
            if inc:
                i.then_inc(self._sem(inc), 1)
        if inc:
            self.n[inc] += 1
            return self.n[inc]
        return None

    def act(self, out, in_, func, bias=0.0, scale=1.0):
        if self.cur == "scalar":
            self.e.activation(out, in_, func, bias=bias, scale=scale).then_inc(self._sem("AC"), 1)
        self.n["AC"] += 1
        return self.n["AC"]

    def dve(self, op, *args, **kw):
        if self.cur == "vector":
            getattr(self.e, op)(*args, **kw).then_inc(self._sem("DV"), 1)
        self.n["DV"] += 1
        return self.n["DV"]

    def coll(self, kind, in_ap, out_ap, sem):
        if self.cur == "gpsimd":
            self.e.collective_compute(
                kind, ALU.bypass,
                replica_groups=[[0, 1, 2, 3], [4, 5, 6, 7]],
                ins=[in_ap], outs=[out_ap],
            ).then_inc(self._sem(sem))

    def barrier(self, extra_cc=()):
        for s in ("MM", "AC", "DV", "MF"):
            if self.n[s]:
                self.e.wait_ge(self._sem(s), self.n[s])
        for s, eng in DMA_SEMS.items():
            if self.cur == eng and self.n[s]:
                self.e.wait_ge(self._sem(s), self.n[s])
        if self.cur == "gpsimd":
            for cc in extra_cc:
                self.e.wait_ge(self._sem(cc), 1)
        self.e.sem_inc(self._sem("PH"), 1)
        self.n["PH"] += 5
        self.e.wait_ge(self._sem("PH"), self.n["PH"])

    def ln_stats(self, x_dram):
        H = self.H
        pb = H["pb"]
        xr, sq, xb = H["xr4"], H["sq"], H["xb"]
        ld = [0] * 32
        acm = [0] * 32
        dvm = [0] * 32
        mmg = [0] * 32
        for c in range(32):
            if c >= 4:
                self.wait(SY, "AC", acm[c - 4])
                self.wait(SY, "DV", dvm[c - 4])
            ld[c] = self.dma(SY, xr[c % 4], x_dram[128 * c:128 * (c + 1), :], f"XR{c % 4}")
            self.wait("scalar", f"XR{c % 4}", ld[c])
            if c >= 2:
                self.wait("scalar", "MM", mmg[c - 2])
            acm[c] = self.act(sq[c % 2], xr[c % 4], AF.Square)
            self.wait("vector", f"XR{c % 4}", ld[c])
            if c >= 2:
                self.wait("vector", "MM", mmg[c - 2])
            dvm[c] = self.dve("tensor_copy", xb[c % 2], xr[c % 4])
            self.wait("tensor", "DV", dvm[c])
            self.wait("tensor", "AC", acm[c])
            for tt in range(2):
                sl = slice(512 * tt, 512 * (tt + 1))
                self.mm(pb[tt][0:1, :], H["ones_col"], xb[c % 2][:, sl],
                        start=(c == 0), stop=(c == 31), inc="MM")
                mmg[c] = self.mm(pb[2 + tt][0:1, :], H["ones_col"], sq[c % 2][:, sl],
                                 start=(c == 0), stop=(c == 31), inc="MM")
        rm, rv, ra, rb = H["row_m"], H["row_v"], H["row_a"], H["row_b"]
        self.wait("scalar", "MM", mmg[31])
        a_rows = 0
        for tt in range(2):
            sl = slice(512 * tt, 512 * (tt + 1))
            self.act(rm[0:1, sl], pb[tt][0:1, :], AF.Copy, scale=1.0 / 4096.0)
            a_rows = self.act(rv[0:1, sl], pb[2 + tt][0:1, :], AF.Copy, scale=1.0 / 4096.0)
        self.wait("vector", "AC", a_rows)
        d_msq = self.dve("tensor_mul", rb[0:1, :], rm[0:1, :], rm[0:1, :])
        self.wait("vector", "DV", d_msq)
        d_veps = self.dve("scalar_tensor_tensor", ra[0:1, :], rv[0:1, :], EPS, rb[0:1, :],
                          ALU.add, ALU.subtract)
        self.wait("scalar", "DV", d_veps)
        a_ln = self.act(rb[0:1, :], ra[0:1, :], AF.Ln)
        self.wait("scalar", "AC", a_ln)
        a_rstd = self.act(ra[0:1, :], rb[0:1, :], AF.Exp, scale=-0.5)
        self.wait("vector", "AC", a_rstd)
        d_b = self.dve("scalar_tensor_tensor", rb[0:1, :], rm[0:1, :], -1.0, ra[0:1, :],
                       ALU.mult, ALU.mult)
        self.wait("tensor", "DV", d_b)
        for tt in range(2):
            sl = slice(512 * tt, 512 * (tt + 1))
            self.mm(pb[4 + tt][:, :], H["ones_row"][0:1, :], ra[0:1, sl],
                    start=True, stop=True, inc="MM")
            self.mm(pb[6 + tt][:, :], H["ones_row"][0:1, :], rb[0:1, sl],
                    start=True, stop=True, inc="MM")
        return self.n["MM"]

    def ln_apply(self, x_dram, bcast_mm):
        H = self.H
        pb = H["pb"]
        xr, tmp, xlnv = H["xr4"], H["dtmp"], H["xln_v"]
        dvm = [0] * 32
        for c in range(32):
            if c >= 4:
                self.wait(SY, "DV", dvm[c - 4])
            ld = self.dma(SY, xr[c % 4], x_dram[128 * c:128 * (c + 1), :], f"XR{c % 4}")
            self.wait("vector", f"XR{c % 4}", ld)
            if c == 0:
                self.wait("vector", "MM", bcast_mm)
            for tt in range(2):
                sl = slice(512 * tt, 512 * (tt + 1))
                d_ = self.dve("tensor_mul", tmp[tt], xr[c % 4][:, sl], pb[4 + tt][:, :])
                self.wait("vector", "DV", d_)
                dvm[c] = self.dve("tensor_add", xlnv[:, c, sl], tmp[tt], pb[6 + tt][:, :])

    def ffn_a(self, app):
        H = self.H
        pb = H["pb"]
        xlnv = H["xln_v"]
        self.dma(GP, H["a1sb"], H["a1_d"], "AL")
        self.dma(GP, H["a3sb"], H["a3_d"], "AL")
        al = self.dma(GP, H["a2sb"], H["a2_d"], "AL")
        self.wait("tensor", "AL", al)
        for tt in range(2):
            sl = slice(512 * tt, 512 * (tt + 1))
            for c in range(32):
                self.mm(pb[0 + tt][0:16, :], H["a1sb"][:, c, :], xlnv[:, c, sl],
                        start=(c == 0), stop=(c == 31), inc="MM" if c == 31 else None)
                self.mm(pb[2 + tt][0:16, :], H["a3sb"][:, c, :], xlnv[:, c, sl],
                        start=(c == 0), stop=(c == 31), inc="MM" if c == 31 else None)
        z_mm = self.n["MM"]
        self.wait("scalar", "MM", z_mm)
        z_ac = 0
        for tt in range(2):
            sl = slice(512 * tt, 512 * (tt + 1))
            self.act(H["z1sb"][0:16, sl], pb[0 + tt][0:16, :], AF.Copy, scale=2.0)
            z_ac = self.act(H["z3sb"][0:16, sl], pb[2 + tt][0:16, :], AF.Copy, scale=2.0)

        w1sb, w3sb, gsb = H["w1sb"], H["w3sb"], H["gsb"]
        b1r, b3r = H["b1r"], H["b3r"]
        wa = [0] * NHC
        ga = [0] * NHC
        grp_h1 = [[0, 0] for _ in range(NHC)]
        grp_h3 = [[0, 0] for _ in range(NHC)]
        grp4 = [0] * NHC
        silu_m = [[0, 0] for _ in range(NHC)]
        dvg = [0] * NHC

        for h in range(NHC):
            p = h % 2
            if h >= 2:
                self.wait(GP, "MM", grp4[h - 2])
                if ga[h - 2]:
                    self.wait(GP, f"GA{h % 2}", ga[h - 2])
            if app == 1:
                self.dma(GP, w1sb[p], H["w1t_d"][h], f"WA{p}")
                self.dma(GP, w3sb[p], H["w3t_d"][h], f"WA{p}")
            else:
                self.dma(GP, w1sb[p], H["w1b_d"][h], f"WA{p}")
                self.dma(GP, w3sb[p], H["w3b_d"][h], f"WA{p}")
            self.dma(GP, b1r[p][0:16, :], H["b1_d"][h], f"WA{p}")
            wa[h] = self.dma(GP, b3r[p][0:16, :], H["b3_d"][h], f"WA{p}")
            if h >= 1:
                q = (h - 1) % 2
                if app == 1:
                    self.wait(GP, f"WA{q}", wa[h - 1])
                    self.dma(GP, H["w1b_d"][h - 1], w1sb[q], f"GA{q}")
                    self.dma(GP, H["w3b_d"][h - 1], w3sb[q], f"GA{q}")
                self.wait(GP, "DV", dvg[h - 1])
                ga[h - 1] = self.dma(GP, H["g_d"][h - 1], gsb[q], f"GA{q}")

            if h >= 1:
                self.wait("tensor", "DV", dvg[h - 1])
                for tt in range(2):
                    self.mm(pb[6 + tt][0:16, :], H["a2sb"][:, h - 1, :],
                            gsb[(h - 1) % 2][:, 512 * tt:512 * (tt + 1)],
                            start=(h - 1 == 0), stop=False)
            self.wait("tensor", f"WA{p}", wa[h])
            bank = [pb[(4 * h + i) % 6] for i in range(4)]
            if h == 0:
                self.wait("tensor", "AC", z_ac)
            if h >= 2:
                self.wait("tensor", "DV", dvg[h - 2])
            if h >= 1:
                self.wait("tensor", "AC", silu_m[h - 1][1])
            for tt in range(2):
                sl = slice(512 * tt, 512 * (tt + 1))
                for c in range(32):
                    self.mm(bank[tt][:, :], w1sb[p][:, c, :], xlnv[:, c, sl],
                            start=(c == 0), stop=False)
                grp_h1[h][tt] = self.mm(bank[tt][:, :], b1r[p][0:16, :],
                                        H["z1sb"][0:16, sl], start=False, stop=True, inc="MM")
            for tt in range(2):
                sl = slice(512 * tt, 512 * (tt + 1))
                for c in range(32):
                    self.mm(bank[2 + tt][:, :], w3sb[p][:, c, :], xlnv[:, c, sl],
                            start=(c == 0), stop=False)
                grp_h3[h][tt] = self.mm(bank[2 + tt][:, :], b3r[p][0:16, :],
                                        H["z3sb"][0:16, sl], start=False, stop=True, inc="MM")
            grp4[h] = grp_h3[h][1]

            for tt in range(2):
                self.wait("scalar", "MM", grp_h1[h][tt])
                if h >= 1:
                    self.wait("scalar", "DV", dvg[h - 1])
                silu_m[h][tt] = self.act(H["ssb"][tt], bank[tt][:, :], AF.Silu)

            for tt in range(2):
                self.wait("vector", "AC", silu_m[h][tt])
                self.wait("vector", "MM", grp_h3[h][tt])
                if h >= 2 and tt == 0:
                    self.wait("vector", "MM", grp4[h - 1])
                    self.wait("vector", f"GA{h % 2}", ga[h - 2])
                dvg[h] = self.dve("tensor_mul", gsb[p][:, 512 * tt:512 * (tt + 1)],
                                  H["ssb"][tt], bank[2 + tt][:, :])

        self.wait("tensor", "DV", dvg[NHC - 1])
        z2_mm = 0
        for tt in range(2):
            z2_mm = self.mm(pb[6 + tt][0:16, :], H["a2sb"][:, NHC - 1, :],
                            gsb[(NHC - 1) % 2][:, 512 * tt:512 * (tt + 1)],
                            start=False, stop=True, inc="MM")
        q = (NHC - 1) % 2
        if app == 1:
            self.wait(GP, f"WA{q}", wa[NHC - 1])
            self.dma(GP, H["w1b_d"][NHC - 1], w1sb[q], f"GA{q}")
            self.dma(GP, H["w3b_d"][NHC - 1], w3sb[q], f"GA{q}")
        self.wait(GP, "DV", dvg[NHC - 1])
        self.dma(GP, H["g_d"][NHC - 1], gsb[q], f"GA{q}")
        self.wait("scalar", "MM", z2_mm)
        for tt in range(2):
            self.act(H["z2sb"][0:16, 512 * tt:512 * (tt + 1)], pb[6 + tt][0:16, :],
                     AF.Copy, scale=2.0)

    def ffn_b(self, app, x_in_dram, x_out_dram):
        H = self.H
        pb = H["pb"]
        w2r, gt = H["w2r"], H["gtr"]
        xr, dtmp, xout = H["xr"], H["dtmp"], H["xout"]
        NI = NHC // 2
        wb = [0] * (8 * NI)
        gl = [0] * (8 * NI)
        mf = [0] * (8 * NI)
        bb = [0] * 8
        b2m = [0] * 8
        evac_ac = [[0] * 8 for _ in range(8)]
        dv_add = {}
        xo = {}
        pend = {}
        for dg in range(8):
            if dg >= 2:
                self.wait(GP, "MM", b2m[dg - 2])
            bb[dg] = self.dma(GP, H["b2r"][dg % 2][0:16, :, :],
                              H["b2_d"][:, 4 * dg:4 * (dg + 1), :], f"BB{dg % 2}")
            for hh in range(NI):
                it = dg * NI + hh
                p = it % 4
                if it >= 4:
                    self.wait(GP, "MF", mf[it - 4])
                wb[it] = self.dma(GP, w2r[p],
                                  H["w2t_d"][dg, 2 * hh:2 * hh + 2]
                                  .rearrange("h p j -> p h j"), f"WB{p}")
                if it >= 4:
                    self.wait(SY, "MF", mf[it - 4])
                gl[it] = self.dma(SY, gt[p],
                                  H["g_d"][2 * hh:2 * hh + 2]
                                  .rearrange("h p t -> p h t"), f"GL{p}")
                self.wait("tensor", f"WB{p}", wb[it])
                self.wait("tensor", f"GL{p}", gl[it])
                for hi in range(2):
                    if hh == 0 and hi == 0 and dg >= 1:
                        for k in range(4):
                            for tt in range(2):
                                self.wait("tensor", "AC", evac_ac[dg - 1][2 * k + tt])
                                self.mm(pb[2 * k + tt][:, :], w2r[p][:, hi, 128 * k:128 * (k + 1)],
                                        gt[p][:, hi, 512 * tt:512 * (tt + 1)],
                                        start=True, stop=False, inc=None)
                        continue
                    for k in range(4):
                        for tt in range(2):
                            r = self.mm(pb[2 * k + tt][:, :], w2r[p][:, hi, 128 * k:128 * (k + 1)],
                                        gt[p][:, hi, 512 * tt:512 * (tt + 1)],
                                        start=(hh == 0 and hi == 0), stop=False,
                                        inc="MF" if (hi == 1 and k == 3 and tt == 1) else None)
                            if r:
                                mf[it] = r
            self.wait("tensor", f"BB{dg % 2}", bb[dg])
            stop_m = [0] * 8
            for k in range(4):
                for tt in range(2):
                    stop_m[2 * k + tt] = self.mm(
                        pb[2 * k + tt][:, :], H["b2r"][dg % 2][0:16, k, :],
                        H["z2sb"][0:16, 512 * tt:512 * (tt + 1)],
                        start=False, stop=True, inc="MM")
            b2m[dg] = stop_m[7]
            for k in range(4):
                kg = 4 * dg + k
                c = kg
                if kg >= 2:
                    self.wait("scalar", "DV", dv_add.get((kg - 2, 1), 0))
                xrl = self.dma("scalar", xr[k % 2], x_in_dram[128 * c:128 * (c + 1), :],
                               f"XP{kg % 2}")
                for tt in range(2):
                    bi = 2 * k + tt
                    self.wait("scalar", "MM", stop_m[bi])
                    if (kg - 1, tt) in pend:
                        self.wait("scalar", "DV", dv_add[(kg - 1, tt)])
                        pc = kg - 1
                        xo[(kg - 1, tt)] = self.dma(
                            "scalar",
                            pend.pop((kg - 1, tt))[128 * pc:128 * (pc + 1),
                                                   512 * tt:512 * (tt + 1)],
                            xout[tt], f"XQ{tt}")
                    a_ = self.act(dtmp[tt], pb[bi][:, :], AF.Copy, scale=0.5)
                    evac_ac[dg][bi] = a_
                    self.wait("vector", "AC", a_)
                    self.wait("vector", f"XP{kg % 2}", xrl)
                    self.wait("vector", f"XQ{tt}", xo.get((kg - 1, tt), 0))
                    dv_add[(kg, tt)] = self.dve("tensor_add", xout[tt], dtmp[tt],
                                                xr[k % 2][:, 512 * tt:512 * (tt + 1)])
                    pend[(kg, tt)] = x_out_dram
        for tt in range(2):
            kg = 31
            self.wait("scalar", "DV", dv_add[(kg, tt)])
            xo[(kg, tt)] = self.dma(
                "scalar", pend.pop((kg, tt))[128 * kg:128 * (kg + 1),
                                             512 * tt:512 * (tt + 1)],
                xout[tt], f"XQ{tt}")
        assert not pend

    def qkv_rope(self, sections=("K", "V", "C", "Q")):
        H = self.H
        pb = H["pb"]
        xlnv = H["xln_v"]
        wsl = H["wslab"]
        cos, sin = H["cos"], H["sin"]

        def proj_group(o, tt, bank, slab):
            sl = slice(512 * tt, 512 * (tt + 1))
            m = None
            for c in range(32):
                m = self.mm(bank[:, :], slab[:, c, :], xlnv[:, c, sl],
                            start=(c == 0), stop=(c == 31),
                            inc="MM" if c == 31 else None)
            return m

        wq = [0] * 48
        pg = [[0, 0] for _ in range(48)]
        rope_dv = [[0, 0] for _ in range(48)]
        kbf_ac = [[0, 0] for _ in range(48)]
        kst_gs = [[0, 0] for _ in range(48)]
        vt_dv = [[0, 0] for _ in range(48)]
        vt_gs = [[0, 0] for _ in range(48)]
        tr_mm = [[0, 0] for _ in range(48)]

        for o in range(8 if "K" in sections else 0):
            p = o % 2
            if o >= 2:
                self.wait(GP, "MM", pg[o - 2][1])
            wq[o] = self.dma(GP, wsl[p], H["wkt_d"][o], f"WQ{p}")
            self.wait("tensor", f"WQ{p}", wq[o])
            for tt in range(2):
                sl = slice(512 * tt, 512 * (tt + 1))
                if o >= 1:
                    self.wait("tensor", "DV", rope_dv[o - 1][tt])
                pg[o][tt] = proj_group(o, tt, pb[tt], wsl[p])
                self.wait("scalar", "MM", pg[o][tt])
                kbf_ac[o][tt] = self.act(H["kbf"][tt], pb[tt][:, :], AF.Copy)
                self.wait("tensor", "AC", kbf_ac[o][tt])
                rot_m = self.mm(pb[2 + tt][:, :], H["prot"], H["kbf"][tt],
                                start=True, stop=True, inc="MM")
                self.wait("vector", "MM", rot_m)
                if o >= 1:
                    self.wait("vector", f"KS{tt}", kst_gs[o - 1][tt])
                self.wait("vector", "DV", self.n["DV"])
                self.dve("tensor_mul", H["t1"], pb[tt][:, :], cos[:, sl])
                d_ = self.dve("tensor_mul", H["t2"], pb[2 + tt][:, :], sin[:, sl])
                self.wait("vector", "DV", d_)
                rope_dv[o][tt] = self.dve("tensor_add", H["kst"][tt], H["t1"], H["t2"])
                self.wait(GP, "DV", rope_dv[o][tt])
                kst_gs[o][tt] = self.dma(GP, H["ksend_d"][128 * o:128 * (o + 1), sl],
                                         H["kst"][tt], f"KS{tt}")

        for oo in range(8 if "V" in sections else 0):
            o = 8 + oo
            p = o % 2
            if oo >= 2:
                self.wait(GP, "MM", pg[o - 2][1])
            wq[o] = self.dma(GP, wsl[p], H["wvt_d"][oo], f"WQ{p}")
            self.wait("tensor", f"WQ{p}", wq[o])
            for tt in range(2):
                if oo >= 1:
                    self.wait("tensor", "AC", kbf_ac[o - 1][tt])
                pg[o][tt] = proj_group(o, tt, pb[tt], wsl[p])
                self.wait("scalar", "MM", pg[o][tt])
                if oo >= 1:
                    self.wait("scalar", "MM", tr_mm[o - 1][tt])
                kbf_ac[o][tt] = self.act(H["vf32"][tt], pb[tt][:, :], AF.Copy)
                self.wait("tensor", "AC", kbf_ac[o][tt])
                prev = vt_dv[o][0] if tt == 1 else (vt_dv[o - 1][1] if oo >= 1 else 0)
                self.wait("tensor", "DV", prev)
                for tau in range(4):
                    tr_mm[o][tt] = self.tr(pb[4][:, 128 * tau:128 * (tau + 1)],
                                           H["vf32"][tt][:, 128 * tau:128 * (tau + 1)],
                                           inc="MM" if tau == 3 else None)
                self.wait("vector", "MM", tr_mm[o][tt])
                if oo >= 1:
                    self.wait("vector", f"VS{tt}", vt_gs[o - 1][tt])
                vt_dv[o][tt] = self.dve("tensor_copy", H["vt"][tt], pb[4][:, :])
                self.wait(GP, "DV", vt_dv[o][tt])
                for tau in range(4):
                    vt_gs[o][tt] = self.dma(
                        GP, H["vtsend_d"][512 * tt + 128 * tau:512 * tt + 128 * (tau + 1),
                                          128 * oo:128 * (oo + 1)],
                        H["vt"][tt][:, 128 * tau:128 * (tau + 1)], f"VS{tt}")

        if "C" in sections and not self.H.get("sim_nocoll"):
            for s in ("KS0", "KS1", "VS0", "VS1"):
                self.wait(GP, s, self.n[s])
            self.coll("AllGather", H["ksend_d"], H["kgath_d"], "CC1")
            self.coll("AllGather", H["vtsend_d"], H["vtgath_d"], "CC2")

        if "Q" in sections:
            kbfq = [[0, 0] for _ in range(32)]
            sqq_ac = [[0, 0] for _ in range(32)]
            rope_mm = [[0, 0] for _ in range(32)]
            q_r2_mm = [[0, 0] for _ in range(32)]
            rt_dv = [0] * 32
            qrope_dv = [[0, 0] for _ in range(32)]

            def q_rot(oo):
                for tt in range(2):
                    self.wait("tensor", "AC", kbfq[oo][tt])
                    if oo >= 1:
                        self.wait("tensor", "DV", qrope_dv[oo - 1][1])
                    elif "V" in sections:
                        self.wait("tensor", "DV", vt_dv[15][1])
                    rope_mm[oo][tt] = self.mm(pb[4 + tt][:, :], H["prot"], H["kbf"][tt],
                                              start=True, stop=True, inc="MM")

            def q_r2(oo):
                for tt in range(2):
                    self.wait("tensor", "AC", sqq_ac[oo][tt])
                    if oo >= 1:
                        self.wait("tensor", "DV", rt_dv[oo - 1])
                    q_r2_mm[oo][tt] = self.mm(pb[6 + tt][0:1, :], H["ones_col"],
                                              H["ssb"][tt], start=True, stop=True, inc="MM")

            def q_rope_dve(oo):
                for tt in range(2):
                    sl = slice(512 * tt, 512 * (tt + 1))
                    self.wait("vector", "MM", rope_mm[oo][tt])
                    self.wait("vector", "DV", self.n["DV"])
                    self.dve("tensor_mul", H["t1"], pb[2 * (oo % 2) + tt][:, :], cos[:, sl])
                    d_ = self.dve("tensor_mul", H["t2"], pb[4 + tt][:, :], sin[:, sl])
                    self.wait("vector", "DV", d_)
                    qrope_dv[oo][tt] = self.dve("tensor_add", H["qp_v"][:, oo, sl],
                                                H["t1"], H["t2"])

            def q_sqq(oo):
                for tt in range(2):
                    self.wait("scalar", "DV", qrope_dv[oo][tt])
                    if oo >= 1:
                        self.wait("scalar", "MM", q_r2_mm[oo - 1][tt])
                    sqq_ac[oo][tt] = self.act(H["ssb"][tt],
                                              H["qp_v"][:, oo, 512 * tt:512 * (tt + 1)],
                                              AF.Square)

            def q_rstat(oo):
                self.wait("vector", "DV", self.n["DV"])
                for tt in range(2):
                    self.wait("vector", "MM", q_r2_mm[oo][tt])
                    self.dve("tensor_reduce", H["rtmp"][0:1, tt:tt + 1],
                             pb[6 + tt][0:1, :], AX.X, ALU.max)
                rt_dv[oo] = self.n["DV"]
                self.wait("vector", "DV", rt_dv[oo])
                self.dve("tensor_max", H["rmax"][0:1, oo:oo + 1], H["rtmp"][0:1, 0:1],
                         H["rtmp"][0:1, 1:2])

            for oo in range(32):
                o = 16 + oo
                p = o % 2
                if oo >= 2:
                    self.wait(GP, "MM", pg[o - 2][1])
                wq[o] = self.dma(GP, wsl[p], H["wqt_d"][oo], f"WQ{p}")
                self.wait("tensor", f"WQ{p}", wq[o])
                for tt in range(2):
                    if oo >= 2:
                        self.wait("tensor", "DV", qrope_dv[oo - 2][tt])
                    elif "V" in sections:
                        self.wait("tensor", "AC", kbf_ac[15][1])
                    pg[o][tt] = proj_group(o, tt, pb[2 * (oo % 2) + tt], wsl[p])
                if oo >= 1:
                    q_rot(oo - 1)
                if oo >= 2:
                    q_r2(oo - 2)
                for tt in range(2):
                    self.wait("scalar", "MM", pg[o][tt])
                    if oo >= 1:
                        self.wait("scalar", "MM", rope_mm[oo - 1][tt])
                    kbfq[oo][tt] = self.act(H["kbf"][tt], pb[2 * (oo % 2) + tt][:, :],
                                            AF.Copy)
                if oo >= 1:
                    q_rope_dve(oo - 1)
                    q_sqq(oo - 1)
                if oo >= 2:
                    q_rstat(oo - 2)
            q_rot(31)
            q_r2(30)
            q_rope_dve(31)
            q_sqq(31)
            q_rstat(30)
            q_r2(31)
            q_rstat(31)

    def attention(self):
        H = self.H
        pb = H["pb"]
        qp = H["qp_v"]
        mend = [0] * 8
        dend = [0] * 8
        aend = [0] * 8
        for hb in range(8):
            par = hb % 2
            ksb, vtsb, vts = H["ksb"][par], H["vtsb"][par], H["vts"][par]
            if hb == 0 and not H.get("sim_nocoll"):
                self.wait(SY, "CC1", 1)
                self.wait(SY, "CC2", 1)
            if hb >= 2:
                self.wait(SY, "MM", mend[hb - 2])
                self.wait(SY, "DV", dend[hb - 2])
                self.wait(SY, "AC", aend[hb - 2])
            kvm = 0
            for c in range(8):
                r, half = min(c, 7 - c), (0 if c <= 3 else 1)
                kvm = self.dma(SY, ksb[:, 512 * c:512 * (c + 1)],
                               H["kgath_d"][1024 * r + 128 * hb:1024 * r + 128 * (hb + 1),
                                            512 * half:512 * (half + 1)], f"KV{par}")
            for c in range(8):
                r, half = min(c, 7 - c), (0 if c <= 3 else 1)
                src = H["vtgath_d"][1024 * r + 512 * half:1024 * r + 512 * (half + 1),
                                    128 * hb:128 * (hb + 1)]
                kvm = self.dma(SY, vtsb[:, 4 * c:4 * (c + 1), :],
                               src.rearrange("(s p) j -> p s j", p=128), f"KV{par}")
            self.wait("vector", f"KV{par}", kvm)
            self.wait("scalar", f"KV{par}", kvm)
            if hb >= 1:
                self.wait("scalar", "MM", mend[hb - 1])
                self.wait("scalar", "DV", dend[hb - 1])
            if hb >= 1:
                self.wait("vector", "MM", mend[hb - 1])
                self.wait("vector", "AC", aend[hb - 1])
            sqk_dv = 0
            for i in range(8):
                sqk_dv = self.dve("tensor_mul", H["sqk"][:, 512 * i:512 * (i + 1)],
                                  ksb[:, 512 * i:512 * (i + 1)],
                                  ksb[:, 512 * i:512 * (i + 1)])
            self.wait("tensor", "DV", sqk_dv)
            r2k_mm = None
            for m in range(32):
                r2k_mm = self.mm(pb[5][:, m:m + 1], H["sqk"][:, 128 * m:128 * (m + 1)],
                                 H["ones_col"], start=True, stop=True,
                                 inc="MM" if m == 31 else None)
            qm_dv = self.dve("tensor_reduce", H["qmaxr"][0:1, 0:1],
                             H["rmax"][0:1, 4 * hb:4 * (hb + 1)], AX.X, ALU.max)
            self.wait("scalar", "DV", qm_dv)
            qs_ac = self.act(H["qsr"][0:1, 0:1], H["qmaxr"][0:1, 0:1], AF.Sqrt)
            self.wait("tensor", "AC", qs_ac)
            qb_mm = self.mm(pb[4][:, 0:1], H["ones_row"][0:1, :], H["qsr"][0:1, 0:1],
                            start=True, stop=True, inc="MM")
            self.wait("scalar", "MM", r2k_mm)
            kabs_ac = self.act(H["kabs"][:, 0:32], pb[5][:, 0:32], AF.Sqrt)
            self.wait("vector", "MM", qb_mm)
            d_qb = self.dve("tensor_copy", H["qbcol"][:, 0:1], pb[4][:, 0:1])
            self.wait("vector", "AC", kabs_ac)
            self.wait("vector", "DV", d_qb)
            negc_dv = self.dve("tensor_scalar", H["negc"][:, 0:32], H["kabs"][:, 0:32],
                               H["qbcol"][:, 0:1], -S128, ALU.mult, ALU.mult)
            self.wait("scalar", "DV", negc_dv)
            self.act(H["expc"][:, 0:32], H["negc"][:, 0:32], AF.Exp, scale=-1.0)
            expc_ac = self.act(H["expcf"][:, 0:32], H["negc"][:, 0:32], AF.Exp, scale=-1.0)
            self.wait("vector", "AC", expc_ac)
            self.wait("vector", f"KV{par}", kvm)
            vts_dv = 0
            for m in range(32):
                vts_dv = self.dve("tensor_scalar", vts[:, m, :], vtsb[:, m, :],
                                  H["expcf"][:, m:m + 1], None, ALU.mult)
            prev_final_dv = vts_dv
            prev2_final_dv = vts_dv
            prev_lb_ac = expc_ac
            prev2_lb_ac = expc_ac
            prev_rec_dv = vts_dv
            prev2_rec_dv = vts_dv
            prev_pair = None
            T_prev = 0
            pi = 0
            for xi, T in ((0, TA), (1, TB)):
                for hq in range(4):
                    q = 4 * hb + hq
                    qsl = slice(512 * xi, 512 * (xi + 1))
                    exp_ac = [0] * T
                    stt_dv = [0] * T
                    qk_mm = [0] * T
                    avl_mm = [0] * T
                    sbank = (pb[0], pb[1], pb[5])
                    ob = pb[2] if pi % 2 == 0 else pb[6]
                    lbk = pb[3] if pi % 2 == 0 else pb[7]
                    def emit_qk(m):
                        if m < 3 and prev_pair is not None:
                            self.wait("tensor", "AC", prev_pair[0][-1])
                        elif m < 3 and prev_pair is None:
                            self.wait("tensor", "AC", prev_lb_ac)
                        qk_mm[m] = self.mm(sbank[m % 3][:, :], ksb[:, 128 * m:128 * (m + 1)],
                                           qp[:, q, qsl], start=True, stop=True, inc="MM")

                    def emit_exp(m):
                        self.wait("scalar", "MM", qk_mm[m])
                        if m >= 3:
                            self.wait("scalar", "DV", stt_dv[m - 3])
                        elif prev_pair is not None:
                            self.wait("scalar", "DV", prev_pair[1][-1])
                        exp_ac[m] = self.act(H["p_sb"][m % 3], sbank[m % 3][:, :], AF.Exp,
                                             bias=H["negc"][:, m:m + 1], scale=S128)

                    def emit_stt(m):
                        self.wait("vector", "AC", exp_ac[m])
                        if m >= 3:
                            self.wait("vector", "MM", avl_mm[m - 3])
                        elif prev_pair is not None:
                            self.wait("vector", "MM", prev_pair[2][-1])
                        stt_dv[m] = self.dve("scalar_tensor_tensor", H["pm_sb"][m % 3],
                                             H["iota"], H["thr"][:, 16 * xi + m:16 * xi + m + 1],
                                             H["p_sb"][m % 3], ALU.is_ge, ALU.mult)

                    def emit_avl(m):
                        self.wait("tensor", "DV", stt_dv[m])
                        if m == 0:
                            self.wait("tensor", "DV", prev2_final_dv)
                            self.wait("tensor", "DV", prev2_rec_dv)
                        self.mm(ob[:, :], vts[:, m, :], H["pm_sb"][m % 3],
                                start=(m == 0), stop=(m == T - 1), inc=None)
                        avl_mm[m] = self.mm(lbk[0:1, :], H["expc"][:, m:m + 1],
                                            H["pm_sb"][m % 3],
                                            start=(m == 0), stop=(m == T - 1), inc="MM")

                    for m0 in range(3):
                        emit_qk(m0)
                        emit_exp(m0)
                        emit_stt(m0)
                    for m in range(T):
                        emit_avl(m)
                        if m + 3 < T:
                            emit_qk(m + 3)
                            emit_exp(m + 3)
                            emit_stt(m + 3)
                    self.wait("vector", "MM", avl_mm[T - 1])
                    self.wait("vector", "MM", mend[hb])
                    rec_dv = self.dve("reciprocal", H["linv"][0:1, :], lbk[0:1, :])
                    self.wait("tensor", "DV", rec_dv)
                    self.wait("tensor", "AC", prev_lb_ac)
                    lb_mm = self.mm(pb[4][:, :], H["ones_row"][0:1, :], H["linv"][0:1, :],
                                    start=True, stop=True, inc="MM")
                    self.wait("scalar", "MM", lb_mm)
                    lb_ac = self.act(H["lb_sb"], pb[4][:, :], AF.Copy)
                    self.wait("vector", "AC", lb_ac)
                    fin_dv = self.dve("tensor_mul", qp[:, q, qsl], ob[:, :], H["lb_sb"])
                    prev2_final_dv, prev_final_dv = prev_final_dv, fin_dv
                    prev2_rec_dv, prev_rec_dv = prev_rec_dv, rec_dv
                    prev2_lb_ac, prev_lb_ac = prev_lb_ac, lb_ac
                    prev_pair = (exp_ac, stt_dv, avl_mm)
                    T_prev = T
                    pi += 1
                    mend[hb] = lb_mm
            dend[hb] = prev_final_dv
            aend[hb] = prev_lb_ac

    def wo_proj(self, x_in_dram, x_out_dram):
        H = self.H
        pb = H["pb"]
        qp = H["qp_v"]
        wsl = H["wosb"]
        xr, dtmp, xout = H["xr"], H["dtmp"], H["xout"]
        wq = [0] * 256
        mf = [0] * 256
        ac8 = [0] * 8
        dv_add = {}
        xo = {}
        for dg in range(8):
            stop_m = [0] * 8
            for o in range(32):
                it = 32 * dg + o
                p = it % 4
                if it >= 4:
                    self.wait(GP, "MF", mf[it - 4])
                wq[it] = self.dma(GP, wsl[p], H["wot_d"][dg, o], f"WQ{p}")
                self.wait("tensor", f"WQ{p}", wq[it])
                if o == 0 and dg >= 1:
                    self.wait("tensor", "AC", ac8[dg - 1])
                for k in range(4):
                    for tt in range(2):
                        inc = "MF" if (k == 3 and tt == 1) else ("MM" if o == 31 else None)
                        r = self.mm(pb[2 * k + tt][:, :], wsl[p][:, k, :],
                                    qp[:, o, 512 * tt:512 * (tt + 1)],
                                    start=(o == 0), stop=(o == 31), inc=inc)
                        if k == 3 and tt == 1:
                            mf[it] = r
                        elif o == 31:
                            stop_m[2 * k + tt] = r
            stop_m[7] = mf[32 * dg + 31]
            for k in range(4):
                kg = 4 * dg + k
                c = kg
                if kg >= 2:
                    self.wait(SY, "DV", dv_add.get((kg - 2, 1), 0))
                xrl = self.dma(SY, xr[k % 2], x_in_dram[128 * c:128 * (c + 1), :],
                               f"XR{kg % 2}")
                for tt in range(2):
                    bi = 2 * k + tt
                    if bi == 7:
                        self.wait("scalar", "MF", stop_m[7])
                    else:
                        self.wait("scalar", "MM", stop_m[bi])
                    self.wait("scalar", "DV", dv_add.get((kg - 1, tt), 0))
                    a_ = self.act(dtmp[tt], pb[bi][:, :], AF.Copy, scale=1.0)
                    self.wait("vector", "AC", a_)
                    self.wait("vector", f"XR{kg % 2}", xrl)
                    self.wait("vector", f"XO{tt}", xo.get((kg - 1, tt), 0))
                    dv_add[(kg, tt)] = self.dve("tensor_add", xout[tt], dtmp[tt],
                                                xr[k % 2][:, 512 * tt:512 * (tt + 1)])
                    self.wait(SY, "DV", dv_add[(kg, tt)])
                    xo[(kg, tt)] = self.dma(
                        SY, x_out_dram[128 * c:128 * (c + 1), 512 * tt:512 * (tt + 1)],
                        xout[tt], f"XO{tt}")
                    ac8[dg] = a_



def build_nc(upto="full", sim_nocoll=False):
    nc = bass.Bass("TRN2", target_bir_lowering=False)

    def din(name, shape):
        return nc.dram_tensor(name, shape, F32, kind="ExternalInput").ap()

    DI = {}
    DI["x0"] = din("x0", [4096, 1024])
    DI["w1t_d"] = din("w1t", [NHC, 128, 32, 128])
    DI["w3t_d"] = din("w3t", [NHC, 128, 32, 128])
    DI["w2t_d"] = din("w2t", [8, NHC, 128, 512])
    DI["wqt_d"] = din("wqt", [32, 128, 32, 128])
    DI["wkt_d"] = din("wkt", [8, 128, 32, 128])
    DI["wvt_d"] = din("wvt", [8, 128, 32, 128])
    DI["wot_d"] = din("wot", [8, 32, 128, 512])
    DI["a1_d"] = din("a1", [128, 32, 16])
    DI["a3_d"] = din("a3", [128, 32, 16])
    DI["a2_d"] = din("a2", [128, NHC, 16])
    DI["b1_d"] = din("b1", [NHC, 16, 128])
    DI["b3_d"] = din("b3", [NHC, 16, 128])
    DI["b2_d"] = din("b2", [16, 32, 128])
    DI["cos_d"] = din("cosd", [128, 1024])
    DI["sin_d"] = din("sind", [128, 1024])
    I16 = mybir.dt.int16
    DI["iota_d"] = nc.dram_tensor("iotad", [128, 512], I16, kind="ExternalInput").ap()
    DI["thr_d"] = nc.dram_tensor("thrd", [128, 48], I16, kind="ExternalInput").ap()
    DI["prot_d"] = din("protd", [128, 128])
    DI["ident_d"] = din("identd", [128, 128])
    DI["onesc_d"] = din("onescd", [128, 1])
    DI["onesr_d"] = din("onesrd", [1, 128])

    y = nc.dram_tensor("y", [4096, 1024], F32, kind="ExternalOutput").ap()
    dbg = None
    if upto != "full":
        dbg = nc.dram_tensor("dbg", [4096, 1024], F32, kind="ExternalOutput").ap()

    x1_d = nc.dram_tensor("x1", [4096, 1024], F32).ap()
    x2_d = nc.dram_tensor("x2", [4096, 1024], F32).ap()
    g_d = nc.dram_tensor("g_s", [NHC, 128, 1024], BF16).ap()
    w1b_d = nc.dram_tensor("w1b", [NHC, 128, 32, 128], BF16).ap()
    w3b_d = nc.dram_tensor("w3b", [NHC, 128, 32, 128], BF16).ap()
    w2b_d = nc.dram_tensor("w2b", [8, NHC, 128, 512], BF16).ap()
    ksend_t = nc.dram_tensor("ksend", [1024, 1024], BF16)
    vtsend_t = nc.dram_tensor("vtsend", [1024, 1024], BF16)
    kgath_t = nc.dram_tensor("kgath", [4096, 1024], BF16)
    vtgath_t = nc.dram_tensor("vtgath", [4096, 1024], BF16)

    with ExitStack() as ctx:
        sb = lambda name, shape, dt: ctx.enter_context(nc.sbuf_tensor(name, shape, dt))
        xln_a = sb("xln_a", [128, 32768], BF16)
        big = sb("big", [128, 32768], BF16)
        aux = sb("aux", [128, 8192], BF16)
        f32a = sb("f32a", [128, 6144], F32)
        cos_t = sb("cos", [128, 1024], F32)
        sin_t = sb("sin", [128, 1024], F32)
        prot_t = sb("prot", [128, 128], BF16)
        ident_t = sb("ident", [128, 128], F32)
        onesc_t = sb("ones_col", [128, 1], BF16)
        onesr_t = sb("ones_row", [1, 128], F32)
        iota_t = sb("iota", [128, 512], mybir.dt.int16)
        thr_t = sb("thr", [128, 48], mybir.dt.int16)
        z1_t = sb("z1sb", [16, 1024], BF16)
        z3_t = sb("z3sb", [16, 1024], BF16)
        z2_t = sb("z2sb", [16, 1024], BF16)
        b1r_t = [sb(f"b1r{i}", [16, 128], BF16) for i in range(2)]
        b3r_t = [sb(f"b3r{i}", [16, 128], BF16) for i in range(2)]
        b2r_t = [sb(f"b2r{i}", [16, 4, 128], BF16) for i in range(2)]
        a1_t = sb("a1sb", [128, 32, 16], BF16)
        a3_t = sb("a3sb", [128, 32, 16], BF16)
        a2_t = sb("a2sb", [128, NHC, 16], BF16)
        ssb_t = [sb(f"ssb{i}", [128, 512], BF16) for i in range(2)]
        kbf_t = [sb(f"kbf{i}", [128, 512], BF16) for i in range(2)]
        kst_t = [sb(f"kst{i}", [128, 512], BF16) for i in range(2)]
        vt_t = [sb(f"vt{i}", [128, 512], BF16) for i in range(2)]
        p_t = [sb(f"p{i}", [128, 512], BF16) for i in range(2)]
        pm_t = [sb(f"pm{i}", [128, 512], BF16) for i in range(2)]
        kabs_t = sb("kabs", [128, 32], F32)
        negc_t = sb("negc", [128, 32], F32)
        expc_t = sb("expc", [128, 32], BF16)
        expcf_t = sb("expcf", [128, 32], F32)
        qbcol_t = sb("qbcol", [128, 1], F32)
        rmax_t = sb("rmax", [1, 32], F32)
        rtmp_t = sb("rtmp", [1, 2], F32)
        qmaxr_t = sb("qmaxr", [1, 1], F32)
        qsr_t = sb("qsr", [1, 1], F32)

        pb = [ctx.enter_context(nc.psum_tensor(f"pb{i}", [128, 512], F32)) for i in range(8)]
        semnames = list(DMA_SEMS) + COMP_SEMS + ["CC1", "CC2"]
        sems = {s: ctx.enter_context(nc.semaphore(s)) for s in semnames}

        H_simflag = sim_nocoll
        bigap = big.ap()
        xlnap = xln_a.ap()
        auxap = aux.ap()
        f32ap = f32a.ap()
        H = dict(
            sim_nocoll=H_simflag,
            sems=sems, pb=pb,
            xln_v=xlnap.rearrange("p (c t) -> p c t", t=1024),
            qp_v=bigap.rearrange("p (o t) -> p o t", t=1024),
            w1sb=[bigap[:, 4096 * i:4096 * (i + 1)].rearrange("p (c j) -> p c j", j=128)
                  for i in range(2)],
            w3sb=[bigap[:, 4096 * (2 + i):4096 * (3 + i)].rearrange("p (c j) -> p c j", j=128)
                  for i in range(2)],
            gsb=[bigap[:, 16384 + 1024 * i:16384 + 1024 * (i + 1)] for i in range(2)],
            w2r=[bigap[:, 1024 * i:1024 * (i + 1)].rearrange("p (h j) -> p h j", j=512)
                 for i in range(4)],
            gtr=[bigap[:, 4096 + 2048 * i:4096 + 2048 * (i + 1)]
                 .rearrange("p (h t) -> p h t", t=1024) for i in range(4)],
            w2sb=[bigap[:, 18432 + 512 * i:18432 + 512 * (i + 1)]
                  .rearrange("p (k j) -> p k j", j=128) for i in range(2)],
            sq=[bigap[:, 19456 + 1024 * i:19456 + 1024 * (i + 1)] for i in range(2)],
            xb=[bigap[:, 21504 + 1024 * i:21504 + 1024 * (i + 1)] for i in range(2)],
            wslab=[auxap[:, 4096 * i:4096 * (i + 1)].rearrange("p (c j) -> p c j", j=128)
                   for i in range(2)],
            wosb=[auxap[:, 512 * i:512 * (i + 1)].rearrange("p (k j) -> p k j", j=128)
                  for i in range(4)],
            ksb=[xlnap[:, 4096 * i:4096 * (i + 1)] for i in range(2)],
            vtsb=[xlnap[:, 8192 + 4096 * i:8192 + 4096 * (i + 1)]
                  .rearrange("p (m j) -> p m j", j=128) for i in range(2)],
            vts=[xlnap[:, 16384 + 4096 * i:16384 + 4096 * (i + 1)]
                 .rearrange("p (m j) -> p m j", j=128) for i in range(2)],
            sqk=xlnap[:, 24576:28672],
            xr=[f32ap[:, 1024 * i:1024 * (i + 1)] for i in range(2)],
            xr4=[f32ap[:, 1024 * i:1024 * (i + 1)] for i in range(4)],
            t1=f32ap[:, 0:512], t2=f32ap[:, 512:1024],
            vf32=[f32ap[:, 1024 + 512 * i:1024 + 512 * (i + 1)] for i in range(2)],
            row_m=f32ap[:, 2048:3072], row_v=f32ap[:, 3072:4096],
            row_a=f32ap[:, 4096:5120], row_b=f32ap[:, 5120:6144],
            dtmp=[f32ap[:, 4096 + 512 * i:4096 + 512 * (i + 1)] for i in range(2)],
            xout=[f32ap[:, 5120 + 512 * i:5120 + 512 * (i + 1)] for i in range(2)],
            lb_sb=f32ap[:, 4096:4608], linv=f32ap[0:1, 2048:2560],
            cos=cos_t.ap(), sin=sin_t.ap(), prot=prot_t.ap(), ident=ident_t.ap(),
            ones_col=onesc_t.ap(), ones_row=onesr_t.ap(),
            iota=iota_t.ap(), thr=thr_t.ap(),
            z1sb=z1_t.ap(), z3sb=z3_t.ap(), z2sb=z2_t.ap(),
            b1r=[x.ap() for x in b1r_t], b3r=[x.ap() for x in b3r_t],
            b2r=[x.ap() for x in b2r_t],
            a1sb=a1_t.ap(), a3sb=a3_t.ap(), a2sb=a2_t.ap(),
            ssb=[x.ap() for x in ssb_t], kbf=[x.ap() for x in kbf_t],
            kst=[x.ap() for x in kst_t], vt=[x.ap() for x in vt_t],
            p_sb=[p_t[0].ap(), p_t[1].ap(), kst_t[0].ap()],
            pm_sb=[pm_t[0].ap(), pm_t[1].ap(), kst_t[1].ap()],
            kabs=kabs_t.ap(), negc=negc_t.ap(),
            expc=expc_t.ap(), expcf=expcf_t.ap(), qbcol=qbcol_t.ap(),
            rmax=rmax_t.ap(), rtmp=rtmp_t.ap(), qmaxr=qmaxr_t.ap(), qsr=qsr_t.ap(),
            g_d=g_d, w1b_d=w1b_d, w3b_d=w3b_d, w2b_d=w2b_d,
            ksend_d=ksend_t.ap(), vtsend_d=vtsend_t.ap(),
            kgath_d=kgath_t.ap(), vtgath_d=vtgath_t.ap(),
            **DI,
        )

        def walk(cur, eng):
            w = Walk(nc, cur, eng, H)
            w.dma(SY, cos_t.ap(), H["cos_d"], "XL")
            w.dma(SY, sin_t.ap(), H["sin_d"], "XL")
            w.dma(SY, iota_t.ap(), H["iota_d"], "XL")
            w.dma(SY, thr_t.ap(), H["thr_d"], "XL")
            w.dma(SY, ident_t.ap(), H["ident_d"], "XL")
            w.dma(SY, onesr_t.ap(), H["onesr_d"], "XL")
            w.dma(GP, prot_t.ap(), H["prot_d"], "WL")
            w.dma(GP, onesc_t.ap(), H["onesc_d"], "WL")
            w.barrier()

            def cut(name, src_dram):
                if upto == name:
                    w.barrier()
                    for c in range(8):
                        w.dma(SY, dbg[512 * c:512 * (c + 1), :],
                              src_dram[512 * c:512 * (c + 1), :], "XS")
                    w.wait(SY, "XS", w.n["XS"])
                    return True
                return False

            def cut_sbuf(name, view3):
                if upto == name:
                    w.barrier()
                    for c in range(32):
                        d = w.dve("tensor_copy", H["xr"][c % 2], view3[:, c, :])
                        w.wait(SY, "DV", d)
                        w.dma(SY, dbg[128 * c:128 * (c + 1), :], H["xr"][c % 2], "XS")
                        w.barrier()
                    w.wait(SY, "XS", w.n["XS"])
                    return True
                return False

            bc = w.ln_stats(H["x0"]); w.barrier()
            w.ln_apply(H["x0"], bc); w.barrier()
            if cut_sbuf("ln1", H["xln_v"]):
                return
            w.ffn_a(1); w.barrier()
            if upto == "gdump":
                for hh in range(32):
                    w.dma(GP, dbg[128 * hh:128 * (hh + 1), :], H["g_d"][hh], "GS")
                w.wait(GP, "GS", w.n["GS"])
                return
            w.ffn_b(1, H["x0"], x1_d); w.barrier()
            if cut("x1", x1_d):
                return
            bc = w.ln_stats(x1_d); w.barrier()
            w.ln_apply(x1_d, bc); w.barrier()
            if cut_sbuf("ln2", H["xln_v"]):
                return
            if upto in ("p7k", "p7v", "p7c", "p7q"):
                secs = {"p7k": ("K",), "p7v": ("K", "V"), "p7c": ("K", "V", "C"),
                        "p7q": ("K", "V", "Q")}[upto]
                w.qkv_rope(sections=secs)
                w.barrier(extra_cc=(("CC1", "CC2") if "C" in secs else ()))
                cut(upto, x1_d)
                return
            w.qkv_rope()
            w.barrier(extra_cc=(() if H.get("sim_nocoll") else ("CC1", "CC2")))
            if cut_sbuf("qkv", H["qp_v"]):
                return
            w.attention(); w.barrier()
            if cut_sbuf("attn", H["qp_v"]):
                return
            w.wo_proj(x1_d, x2_d); w.barrier()
            if cut("x2", x2_d):
                return
            bc = w.ln_stats(x2_d); w.barrier()
            w.ln_apply(x2_d, bc); w.barrier()
            if cut_sbuf("ln3", H["xln_v"]):
                return
            w.ffn_a(2); w.barrier()
            w.ffn_b(2, x2_d, y); w.barrier()
            for s in ("XO0", "XO1"):
                w.wait(SY, s, w.n[s])

        block = ctx.enter_context(nc.Block())
        block.gpsimd(lambda g: walk("gpsimd", g))
        block.sync(lambda s: walk("sync", s))
        block.tensor(lambda t: walk("tensor", t))
        block.scalar(lambda s: walk("scalar", s))
        block.vector(lambda v: walk("vector", v))

    return nc



def _prep_shared(inp):
    f32 = np.float32
    a = lambda k: np.asarray(inp[k], f32)
    W1, W3, W2 = a("L0_ffn_w1"), a("L0_ffn_w3"), a("L0_ffn_w2")
    Wq, Wk, Wv, Wo = a("L0_attn_Wq"), a("L0_attn_Wk"), a("L0_attn_Wv"), a("L0_attn_Wo")
    A1, A3, A2 = a("L0_ffn_lora_A1"), a("L0_ffn_lora_A3"), a("L0_ffn_lora_A2")
    B1, B3, B2 = a("L0_ffn_lora_B1"), a("L0_ffn_lora_B3"), a("L0_ffn_lora_B2")
    C = np.ascontiguousarray
    d = {}
    d["w1t"] = C(W1.reshape(NHC, 128, 32, 128).transpose(0, 3, 2, 1))
    d["w3t"] = C(W3.reshape(NHC, 128, 32, 128).transpose(0, 3, 2, 1))
    d["w2t"] = C(W2.reshape(8, 512, NHC, 128).transpose(0, 2, 3, 1))
    d["wqt"] = C(Wq.reshape(32, 128, 32, 128).transpose(0, 3, 2, 1))
    d["wkt"] = C(Wk.reshape(8, 128, 32, 128).transpose(0, 3, 2, 1))
    d["wvt"] = C(Wv.reshape(8, 128, 32, 128).transpose(0, 3, 2, 1))
    d["wot"] = C(Wo.reshape(8, 512, 32, 128).transpose(0, 2, 3, 1))
    d["a1"] = C(A1.reshape(16, 32, 128).transpose(2, 1, 0))
    d["a3"] = C(A3.reshape(16, 32, 128).transpose(2, 1, 0))
    d["a2"] = C(A2.reshape(16, NHC, 128).transpose(2, 1, 0))
    d["b1"] = C(B1.reshape(NHC, 128, 16).transpose(0, 2, 1))
    d["b3"] = C(B3.reshape(NHC, 128, 16).transpose(0, 2, 1))
    d["b2"] = C(B2.reshape(32, 128, 16).transpose(2, 0, 1))
    p = np.zeros((128, 128), f32)
    for m in range(64):
        p[m + 64, m] = -1.0
        p[m, m + 64] = 1.0
    d["protd"] = p
    d["identd"] = np.eye(128, dtype=f32)
    d["onescd"] = np.ones((128, 1), f32)
    d["onesrd"] = np.ones((1, 128), f32)
    d["iotad"] = np.broadcast_to(np.arange(512, dtype=np.int16), (128, 512)).copy()
    return d


def _prep_core(inp, core):
    f32 = np.float32
    b, j = core // 4, core % 4
    cA, cB = j, 7 - j
    x0 = np.asarray(inp["L0_attn_hidden_states"], f32)
    xc = np.ascontiguousarray(
        np.concatenate([x0[b, 512 * cA:512 * (cA + 1), :],
                        x0[b, 512 * cB:512 * (cB + 1), :]], 0).T)
    pos = np.concatenate([np.arange(512 * cA, 512 * (cA + 1)),
                          np.arange(512 * cB, 512 * (cB + 1))]).astype(f32)
    invf = (1.0 / (10000.0 ** (np.arange(0, 128, 2, dtype=f32) / f32(128.0)))).astype(f32)
    freqs = np.concatenate([invf, invf])[:, None] * pos[None, :]
    d = {"x0": xc,
         "cosd": np.cos(freqs).astype(f32),
         "sind": np.sin(freqs).astype(f32)}
    thr = np.full((128, 48), 32000, np.int64)
    karr = np.arange(128, dtype=np.int64)
    for xi, (qb_, T) in enumerate(((512 * cA, TA), (512 * cB, TB))):
        for m in range(T):
            thr[:, 16 * xi + m] = 128 * m + karr - qb_
    d["thrd"] = thr.astype(np.int16)
    return d


_BUILT = {}


def _get_nc(upto):
    if upto not in _BUILT:
        _BUILT[upto] = build_nc(upto)
    return _BUILT[upto]


def run(inputs, upto="full"):
    shared = _prep_shared(inputs)
    in_maps = []
    for core in range(8):
        m = dict(shared)
        m.update(_prep_core(inputs, core))
        in_maps.append(m)
    return run_bass_kernel_spmd(_get_nc(upto), in_maps, core_ids=list(range(8)))


def assemble(results, key="y"):
    out = np.zeros((2, 4096, 4096), np.float32)
    for core in range(8):
        b, j = core // 4, core % 4
        cA, cB = j, 7 - j
        yc = results[core][key]
        out[b, 512 * cA:512 * (cA + 1), :] = yc[:, 0:512].T
        out[b, 512 * cB:512 * (cB + 1), :] = yc[:, 512:1024].T
    return out


def kernel(**inputs):
    res = run(inputs, upto="full")
    return assemble(res.results, "y")
